# version 09
# baseline (speedup 1.0000x reference)
import contextlib
import numpy as np
import ml_dtypes
import concourse.bass as bass
import concourse.mybir as mybir
from concourse.bass_utils import run_bass_kernel_spmd

F32 = mybir.dt.float32
BF16 = mybir.dt.bfloat16
AF = mybir.ActivationFunctionType
ALU = mybir.AluOpType
AX = mybir.AxisListType

B, S, D = 8, 2048, 2048
NC = 16
NTC = 4
H = 16
FF = 8192
FFC = 64
RLOW = 32
NB = 4
SCALE = float(1.0 / np.sqrt(128.0))
EPS = 1e-5
NEG = -1.0e30

DBG = {"phases": 99, "outs": ()}

OFF_A = 0
OFF_B = 65536
OFF_C = 131072
OFF_D = 152576
OFF_E = 164864
OFF_F = 178176
OFF_G = 186368
OFF_H = 190464
OFF_I = 206848


class Prog:
    ENG = ["pe", "dve", "act", "pool", "sp"]
    POOLN = {"sp": 20, "pool": 20, "act": 12, "dve": 2, "pe": 2}

    def __init__(self):
        self.steps = {e: [] for e in self.ENG}
        self.last_writer = {}
        self.readers = {}
        self.dma_count = {e: 0 for e in self.ENG}
        self.comp_count = {e: 0 for e in self.ENG}
        self.pool_uses = {}

    def step(self, eng, emit, reads=(), writes=(), deps=(), dma=0, tag=""):
        st = {"eng": eng, "emit": emit, "dma": dma, "tag": tag, "waits": {}}
        if dma:
            idx = self.dma_count[eng] % self.POOLN[eng]
            self.dma_count[eng] += 1
            u = self.pool_uses.get((eng, idx), 0) + dma
            self.pool_uses[(eng, idx)] = u
            st["dma_sem"] = (eng, idx)
            st["dma_post"] = u
        else:
            self.comp_count[eng] += 1
            st["comp_post"] = self.comp_count[eng]
        depset = list(deps)
        for k in reads:
            w = self.last_writer.get(k)
            if w is not None:
                depset.append(w)
        for k in writes:
            w = self.last_writer.get(k)
            if w is not None:
                depset.append(w)
            depset.extend(self.readers.get(k, ()))
        for d in depset:
            if d is None or d is st:
                continue
            if d["eng"] == eng and not d["dma"]:
                continue
            if d["dma"]:
                sem, val = ("dmap",) + d["dma_sem"], 16 * d["dma_post"]
            else:
                sem, val = ("comp", d["eng"]), d["comp_post"]
            if st["waits"].get(sem, 0) < val:
                st["waits"][sem] = val
        for k in reads:
            self.readers.setdefault(k, []).append(st)
        for k in writes:
            self.last_writer[k] = st
            self.readers[k] = []
        self.steps[eng].append(st)
        return st


def _emit_engine(prog, eng_name, engobj, sems, waited):
    for st in prog.steps[eng_name]:
        for sem_key in sorted(st["waits"], key=str):
            val = st["waits"][sem_key]
            if waited[eng_name].get(sem_key, 0) >= val:
                continue
            engobj.wait_ge(sems[sem_key], val)
            waited[eng_name][sem_key] = val
        out = st["emit"](engobj)
        if st["dma"]:
            insts = out if isinstance(out, (list, tuple)) else [out]
            assert len(insts) == st["dma"], f"{st['tag']}: dma count mismatch"
            for ins in insts:
                ins.then_inc(sems[("dmap",) + st["dma_sem"]], 16)
        else:
            assert out is not None, f"step {st['tag']} returned None"
            out.then_inc(sems[("comp", eng_name)], 1)


def blk_lhsT(w):
    O, Din = w.shape
    r = w.reshape(O // 128, 128, Din // 128, 128)
    return np.ascontiguousarray(r.transpose(0, 3, 2, 1))


def blk_ff2(w):
    r = w.reshape(16, 128, 64, 128)
    return np.ascontiguousarray(r.transpose(0, 3, 2, 1))


def blk_x(xb):
    r = xb.reshape(4, 512, 16, 128)
    return np.ascontiguousarray(r.transpose(2, 0, 3, 1))


def unblk_out(o):
    return np.ascontiguousarray(o.transpose(1, 3, 0, 2).reshape(S, D))


def pack_vec(v, n):
    return np.ascontiguousarray(v.reshape(n, 128).T)


def build_program():
    nc = bass.Bass("TRN2", target_bir_lowering=False)
    phases = DBG["phases"]

    xT = nc.dram_tensor("xT", [NC, NTC, 128, 512], F32, kind="ExternalInput")
    cattn = nc.dram_tensor("cattn", [18, 128, NC, 128], F32, kind="ExternalInput")
    cproj = nc.dram_tensor("cproj", [NC, 128, NC, 128], F32, kind="ExternalInput")
    wq_i = nc.dram_tensor("wq", [NC, 128, NC, 128], F32, kind="ExternalInput")
    wk_i = nc.dram_tensor("wk", [NC, 128, NC, 128], F32, kind="ExternalInput")
    wv_i = nc.dram_tensor("wv", [NC, 128, NC, 128], F32, kind="ExternalInput")
    wo_i = nc.dram_tensor("wo", [NC, 128, NC, 128], F32, kind="ExternalInput")
    ff1_i = nc.dram_tensor("ff1", [FFC, 128, NC, 128], F32, kind="ExternalInput")
    ff2_i = nc.dram_tensor("ff2", [NC, 128, FFC, 128], F32, kind="ExternalInput")
    wdn_i = nc.dram_tensor("wdn", [128, NC, RLOW], F32, kind="ExternalInput")
    wup_i = nc.dram_tensor("wup", [RLOW, NC, 128], F32, kind="ExternalInput")
    wrt_i = nc.dram_tensor("wrt", [128, NC, 3], F32, kind="ExternalInput")
    vecs = nc.dram_tensor("vecs", [128, 114], F32, kind="ExternalInput")
    outT = nc.dram_tensor("outT", [NC, NB, 128, 512], F32, kind="ExternalOutput")

    def dram(n, s, dt):
        kind = "ExternalOutput" if n in DBG["outs"] else "Internal"
        return nc.dram_tensor(n, s, dt, kind=kind)

    xn_s = dram("xn_s", [NC, NB, 128, 512], BF16)
    A_s = dram("A_s", [NC, NB, 128, 512], BF16)
    k2_s = dram("k2_s", [H, NB, 128, 512], BF16)
    q2_s = dram("q2_s", [H, NB, 128, 512], BF16)
    v2_s = dram("v2_s", [H, NC, 128, 128], BF16)
    ctx2_s = dram("ctx2_s", [H, NC, 128, 128], BF16)
    wo_img = dram("wo_img", [NC, 128, NC, 128], BF16)
    ff1_img = dram("ff1_img", [FFC, 128, NC, 128], BF16)
    ff2_img = dram("ff2_img", [NC, 128, FFC, 128], BF16)
    mrow_s = dram("mrow_s", [4, 2048], F32)
    t_s = dram("t_s", [RLOW, 2048], BF16)
    wupb_s = dram("wupb_s", [RLOW, NC, 128], BF16)

    iden16_h = nc.inline_tensor(np.eye(128, dtype=ml_dtypes.bfloat16), "iden16c")
    tri_np = np.where(np.arange(128)[None, :] <= np.arange(128)[:, None],
                      np.float32(0.0), np.float32(NEG))
    tri_h = nc.inline_tensor(tri_np, "tri_c")
    ones_h = nc.inline_tensor(np.ones((128, 4), np.float32), "ones_c")
    onesr_h = nc.inline_tensor(np.ones((1, 128), np.float32), "onesr_c")
    ones16_h = nc.inline_tensor(np.ones((128, 4), ml_dtypes.bfloat16), "o16_c")
    onesr16_h = nc.inline_tensor(np.ones((1, 128), ml_dtypes.bfloat16), "or16_c")
    onesT16_h = nc.inline_tensor(np.ones((128, 128), ml_dtypes.bfloat16), "oT16_c")
    onesT32_h = nc.inline_tensor(np.ones((128, 128), np.float32), "oT32_c")
    _cn = np.zeros((128, 4), np.float32)
    _cn[:, 0] = EPS
    _cn[:, 1] = SCALE
    _cn[:, 2] = 1.0 / 2048.0
    scalars_h = nc.inline_tensor(_cn, "scalars_c")

    sbA = nc.alloc_sbuf_tensor("sbA", [128, 53200], F32)
    sb16 = sbA.bitcast(BF16)

    def b16(off, cols, rows=128, p0=0):
        return sb16[p0:p0 + rows, off // 2: off // 2 + cols]

    def f32v(off, cols, rows=128, p0=0):
        return sbA[p0:p0 + rows, off // 4: off // 4 + cols]

    I = OFF_I
    iden16 = b16(I, 128)
    tri32 = f32v(I + 256, 128)
    ones16_col = b16(I + 768, 1)
    ones16_col4 = b16(I + 768, 4)
    onesr = f32v(I + 800, 128, rows=1)
    onesr16 = b16(I + 1312, 128, rows=1)
    vec_sb = f32v(I + 1568, 114)
    wrt_sb = f32v(I + 2024, 48)
    wdn_sb = b16(I + 2216, 512)
    small = f32v(I + 3240, 16)
    onesT16 = b16(I + 3328, 128)
    onesT32 = f32v(I + 3584, 128)
    junkb = b16(I + 4096, 128)
    sqb = f32v(I + 4352, 18)
    pmax_mq = f32v(I + 4480, 4)
    pmax_in = f32v(I + 4496, 4)

    def onesr16_at(p0):
        return onesT16[p0:p0 + 1, :]

    def onesr_at(p0):
        return onesT32[p0:p0 + 1, :]
    V_CATTN, V_CPROJ, V_SB, V_L1S, V_L1B, V_L2S, V_L2B = 0, 18, 34, 50, 66, 82, 98

    pscore = nc.alloc_psum_tensor("pscore", [128, 2048], F32)
    pacc = nc.alloc_psum_tensor("pacc", [128, 512], F32)
    putil = nc.alloc_psum_tensor("putil", [128, 512], F32)
    ptr = nc.alloc_psum_tensor("ptr", [128, 1024], BF16)
    pacc2 = nc.alloc_psum_tensor("pacc2", [128, 512], F32)

    def psv(i):
        return pscore[:, 512 * i:512 * (i + 1)], f"PS{i}"

    epsc = small[:, 4:5]
    epsr = small[0:1, 4:5]
    sclc = small[:, 5:6]
    inv_n = small[:, 6:7]
    inv_nr = small[0:1, 6:7]

    P = Prog()
    MM = dict(skip_group_check=True)

    def _ld_consts(sy):
        return [sy.dma_start(iden16, iden16_h[:, :]),
                sy.dma_start(tri32, tri_h[:, :]),
                sy.dma_start(ones16_col4, ones16_h[:, :]),
                sy.dma_start(onesr, onesr_h[:, :]),
                sy.dma_start(onesr16, onesr16_h[:, :]),
                sy.dma_start(onesT16, onesT16_h[:, :]),
                sy.dma_start(onesT32, onesT32_h[:, :]),
                sy.dma_start(vec_sb, vecs[:, :]),
                sy.dma_start(wrt_sb, wrt_i.rearrange("p c r -> p (c r)")),
                sy.dma_start(small[:, 4:8], scalars_h[:, :])]
    P.step("sp", _ld_consts, dma=10, writes=["consts", "epsc"], tag="consts")
    P.step("pool", lambda gp: gp.dma_start(
        wdn_sb, wdn_i.rearrange("p c r -> p (c r)")),
        dma=1, writes=["wdn"], tag="wdn")
    P.step("pool", lambda gp: gp.dma_start(wupb_s[:, :, :], wup_i[:, :, :]),
           dma=1, writes=["wup_s"], tag="wupcast")
    P.step("dve", (lambda ve: ve.tensor_scalar(
        sqb[:, 0:16], vec_sb[:, 0:16], sclc, None, ALU.mult)),
        reads=["consts", "epsc"], writes=["sqb"], tag="sqb")

    x16a = [b16(OFF_B, 2048), b16(OFF_B + 4096, 2048)]
    xsq16 = b16(OFF_B + 8192, 2048)
    xrow = b16(OFF_B + 8192, 2048, rows=1)
    xf = [f32v(OFF_B + 16384, 512), f32v(OFF_B + 18432, 512)]
    xnf = [f32v(OFF_B + 20480, 512), f32v(OFF_B + 22528, 512)]
    tmid = b16(OFF_B + 24576, 2048, rows=RLOW)
    mu_bc = f32v(OFF_G, 512)
    rs_bc = f32v(OFF_G + 2048, 512)
    r_mu = b16(OFF_H, 2048, rows=1)
    r_var = b16(OFF_H + 4096, 2048, rows=1)
    r_rstd = b16(OFF_H + 8192, 2048, rows=1)
    r_rstd1 = b16(OFF_H + 12288, 2048, rows=1)
    r_mu2 = f32v(OFF_H, 512, rows=1)
    r_rs2 = f32v(OFF_H + 2048, 512, rows=1)

    def xn_slot(dc):
        return b16(OFF_A + 4096 * dc, 2048), f"A{dc}"

    def q_slot(oc):
        return b16(OFF_B + 4096 * oc, 2048), f"B{oc}"

    AKEYS = [f"A{d}:{t}" for d in range(NC) for t in range(NTC)]

    if phases >= 1:
        for dc in range(NC):
            sl = dc % 2
            P.step("pool", (lambda gp, dc=dc, sl=sl: gp.dma_start(
                x16a[sl].rearrange("p (t f) -> p t f", t=4),
                xT.rearrange("c t p f -> c p t f")[dc])),
                dma=1, writes=[f"x16:{sl}"], tag=f"p1a_ld{dc}")
            P.step("act", (lambda sc, sl=sl: sc.activation(
                xsq16, x16a[sl], AF.Square)),
                reads=[f"x16:{sl}"], writes=["xsq"], tag=f"p1a_sq{dc}")

            def mmx(te, dc=dc, sl=sl):
                last = None
                for tc in range(NTC):
                    last = te.matmul(pscore[0:1, 512 * tc:512 * tc + 512],
                                     ones16_col, x16a[sl][:, 512 * tc:512 * tc + 512],
                                     start=(dc == 0), stop=(dc == NC - 1), **MM)
                return last
            P.step("pe", mmx, reads=[f"x16:{sl}", "consts"],
                   writes=(["PS0", "PS1", "PS2", "PS3"]
                           if dc in (0, NC - 1) else []), tag=f"p1a_mmx{dc}")

            def mmx2(te, dc=dc):
                dsts = [putil[0:1, :], pacc[0:1, :], pacc2[0:1, :],
                        ptr.bitcast(F32)[0:1, 0:512]]
                last = None
                for tc in range(NTC):
                    last = te.matmul(dsts[tc], ones16_col,
                                     xsq16[:, 512 * tc:512 * tc + 512],
                                     start=(dc == 0), stop=(dc == NC - 1), **MM)
                return last
            P.step("pe", mmx2, reads=["xsq"],
                   writes=(["PUTIL", "PACC", "PACC2", "PTR"]
                           if dc in (0, NC - 1) else []), tag=f"p1a_mmq{dc}")

        def _stats(ve):
            for tc in range(NTC):
                ve.tensor_scalar(r_mu[:, 512 * tc:512 * tc + 512],
                                 pscore[0:1, 512 * tc:512 * tc + 512],
                                 inv_nr, None, ALU.mult)
            srcs = [putil, pacc, pacc2, ptr.bitcast(F32)]
            for tc in range(NTC):
                ve.tensor_scalar(r_var[:, 512 * tc:512 * tc + 512],
                                 srcs[tc][0:1, 0:512], inv_nr, None, ALU.mult)
            ve.tensor_tensor(xrow, r_mu, r_mu, ALU.mult)
            ve.tensor_copy(junkb, iden16)
            ve.tensor_copy(junkb, iden16)
            return ve.tensor_tensor(r_var, r_var, xrow, ALU.subtract)
        P.step("dve", _stats,
               reads=["PS0", "PS1", "PS2", "PS3", "PUTIL", "PACC", "PACC2",
                      "PTR", "epsc"],
               writes=["r_mu", "r_var", "xsq"], tag="p1_stats")

        def _rstd(sc):
            sc.activation(xrow, r_var, AF.Ln, bias=epsr)
            sc.copy(junkb, iden16)
            sc.copy(junkb, iden16)
            return sc.activation(r_rstd, xrow, AF.Exp, scale=-0.5)
        P.step("act", _rstd, reads=["r_var", "epsc"], writes=["r_rstd", "xsq"],
               tag="p1_rstd")

        def _v1(ve):
            ve.tensor_tensor(xrow, r_rstd, r_rstd, ALU.mult)
            ve.tensor_copy(junkb, iden16)
            ve.tensor_copy(junkb, iden16)
            return ve.tensor_tensor(r_rstd1, r_var, xrow, ALU.mult)
        P.step("dve", _v1, reads=["r_rstd", "r_var"],
               writes=["r_rstd1", "xsq"], tag="p1_v1")

        def _rstd1(sc):
            sc.activation(xrow, r_rstd1, AF.Ln, bias=epsr)
            sc.copy(junkb, iden16)
            sc.copy(junkb, iden16)
            return sc.activation(r_rstd1, xrow, AF.Exp, scale=-0.5)
        P.step("act", _rstd1, reads=["r_rstd1", "epsc"],
               writes=["r_rstd1", "xsq"], tag="p1_rstd1")

        for tc in range(NTC):
            def _bc(te, tc=tc):
                te.matmul(putil[:, :], onesr16_at(0),
                          r_mu[:, 512 * tc:512 * tc + 512], start=True,
                          stop=True, **MM)
                return te.matmul(pacc[:, :], onesr16_at(0),
                                 r_rstd[:, 512 * tc:512 * tc + 512],
                                 start=True, stop=True, **MM)
            P.step("pe", _bc, reads=["r_mu", "r_rstd", "consts"],
                   writes=["PUTIL", "PACC"], tag=f"p1b_bc{tc}")

            def _bce(ve):
                ve.tensor_copy(mu_bc, putil[:, :])
                return ve.tensor_copy(rs_bc, pacc[:, :])
            P.step("dve", _bce, reads=["PUTIL", "PACC"],
                   writes=["mu_bc", "rs_bc"], tag=f"p1b_bce{tc}")

            for dc in range(NC):
                sl = dc % 2
                P.step("sp", (lambda sy, dc=dc, tc=tc, sl=sl: sy.dma_start(
                    xf[sl], xT[dc, tc])),
                    dma=1, writes=[f"xf:{sl}"], tag=f"p1b_ld{dc}_{tc}")

                def _xn(ve, dc=dc, tc=tc, sl=sl):
                    ve.tensor_tensor(xnf[sl], xf[sl], mu_bc, ALU.subtract)
                    ve.tensor_tensor(xnf[sl], xnf[sl], rs_bc, ALU.mult)
                    xn, _ = xn_slot(dc)
                    return ve.tensor_copy(xn[:, 512 * tc:512 * tc + 512],
                                          xnf[sl])
                P.step("dve", _xn, reads=[f"xf:{sl}", "mu_bc", "rs_bc"],
                       writes=[f"xnf:{sl}", f"A{dc}:{tc}"],
                       tag=f"p1b_xn{dc}_{tc}")

                def _rt(te, dc=dc, tc=tc, sl=sl):
                    return te.matmul(pscore[0:3, 512 * tc:512 * tc + 512],
                                     wrt_sb[:, 3 * dc:3 * dc + 3], xnf[sl],
                                     start=(dc == 0), stop=(dc == NC - 1),
                                     **MM)
                P.step("pe", _rt, reads=[f"xnf:{sl}", "consts"],
                       writes=([f"PS{tc}"] if dc in (0, NC - 1) else []),
                       tag=f"p1b_rt{dc}_{tc}")

        for dc in range(NC):
            xn, _ = xn_slot(dc)
            P.step("pool", (lambda gp, dc=dc, xn=xn: [
                gp.dma_start(xn_s[dc, t], xn[:, 512 * t:512 * t + 512])
                for t in range(NB)]),
                dma=NB, reads=[f"A{dc}:{t}" for t in range(NTC)],
                writes=[f"xn_s:{dc}"], tag=f"p1_spxn{dc}")

        for tc in range(NTC):
            def _lr(te, tc=tc):
                last = None
                for dc in range(NC):
                    xn, _ = xn_slot(dc)
                    last = te.matmul(putil[0:RLOW, :],
                                     wdn_sb[:, RLOW * dc:RLOW * dc + RLOW],
                                     xn[:, 512 * tc:512 * tc + 512],
                                     start=(dc == 0), stop=(dc == NC - 1), **MM)
                return last
            P.step("pe", _lr,
                   reads=["wdn"] + [f"A{d}:{tc}" for d in range(NC)],
                   writes=["PUTIL"], tag=f"p1_lr{tc}")
            P.step("dve", (lambda ve, tc=tc: ve.tensor_copy(
                tmid[:, 512 * tc:512 * tc + 512], putil[0:RLOW, :])),
                reads=["PUTIL"], writes=[f"tmid:{tc}", "x16:0", "x16:1"],
                tag=f"p1_lre{tc}")
        P.step("pool", (lambda gp: gp.dma_start(t_s[:, :], tmid)),
               dma=1, reads=[f"tmid:{t}" for t in range(NTC)], writes=["t_s"],
               tag="p1_spt")

        lrow = f32v(OFF_B + 32768, 2048, rows=3)
        l0 = f32v(OFF_B + 40960, 2048, rows=1)
        l1 = f32v(OFF_B + 49152, 2048, rows=1)
        l2 = f32v(OFF_B + 57344, 2048, rows=1)
        m0 = f32v(OFF_B, 2048, rows=1)
        m1 = f32v(OFF_B + 8192, 2048, rows=1)
        m2 = f32v(OFF_B + 16384, 2048, rows=1)
        tt1 = f32v(OFF_B + 24576, 2048, rows=1)

        def _lev(ve):
            last = None
            for tc in range(NTC):
                last = ve.tensor_copy(lrow[:, 512 * tc:512 * tc + 512],
                                      pscore[0:3, 512 * tc:512 * tc + 512])
            return last
        P.step("dve", _lev, reads=["PS0", "PS1", "PS2", "PS3"],
               writes=["lrow"], tag="p1_lev")
        P.step("pool", (lambda gp: [gp.dma_start(l0, lrow[0:1, :]),
                                    gp.dma_start(l1, lrow[1:2, :]),
                                    gp.dma_start(l2, lrow[2:3, :])]),
               dma=3, reads=["lrow"], writes=["lr"], tag="p1_lmv")

        c_rows = [f32v(OFF_B + 8192 * i, 2048, rows=1, p0=64) for i in range(6)]

        def _mask(ve):
            ve.tensor_tensor(c_rows[0], l0, l1, ALU.is_ge)
            ve.tensor_tensor(c_rows[1], l0, l2, ALU.is_ge)
            ve.tensor_tensor(c_rows[2], l1, l0, ALU.is_gt)
            ve.tensor_tensor(c_rows[3], l1, l2, ALU.is_ge)
            ve.tensor_tensor(c_rows[4], l2, l0, ALU.is_gt)
            ve.tensor_tensor(c_rows[5], l2, l1, ALU.is_gt)
            ve.tensor_tensor(m0, c_rows[0], c_rows[1], ALU.mult)
            ve.tensor_tensor(m1, c_rows[2], c_rows[3], ALU.mult)
            return ve.tensor_tensor(m2, c_rows[4], c_rows[5], ALU.mult)
        P.step("dve", _mask, reads=["lr"],
               writes=["mrows", "x16:0", "x16:1", "xsq", "tmid:0", "tmid:1",
                       "tmid:2", "tmid:3"], tag="p1_mask")
        P.step("pool", (lambda gp: [gp.dma_start(mrow_s[0:1, :], m0),
                                    gp.dma_start(mrow_s[1:2, :], m1),
                                    gp.dma_start(mrow_s[2:3, :], m2)]),
               dma=3, reads=["mrows"], writes=["mrow_s"], tag="p1_msp")

    wband = [b16(OFF_D + 4096 * i, 2048) for i in range(3)]
    s32 = [f32v(OFF_E + 2048 * i, 512) for i in range(2)]
    vstage = b16(OFF_C + 5120, 2048)
    vtstage = b16(OFF_E + 8192, 512)
    k_sb = b16(OFF_F, 2048)
    v_tok = b16(OFF_F + 4096, 2048)

    def band_load(img, oc, cast, tag, reads=(), extra_writes=()):
        slot = oc % 3
        eng = "pool" if cast else "sp"
        P.step(eng, (lambda e, img=img, oc=oc, slot=slot: e.dma_start(
            wband[slot], img.rearrange("o p c f -> o p (c f)")[oc])),
            dma=1, reads=list(reads),
            writes=[f"wband:{slot}"] + list(extra_writes), tag=tag)
        return slot

    def proj_pair(tag, oc, slot, rhs_of, rhs_keys, regions, sink, pair):
        def _ch(te, pair=pair, slot=slot):
            last = None
            for dc in range(NC):
                rhs = rhs_of(dc)
                for i in range(2):
                    tc = 2 * pair + i
                    last = te.matmul(regions[i][0],
                                     wband[slot][:, 128 * dc:128 * dc + 128],
                                     rhs[:, 512 * tc:512 * tc + 512],
                                     start=(dc == 0), stop=(dc == NC - 1),
                                     **MM)
            return last
        P.step("pe", _ch, reads=[f"wband:{slot}"] + rhs_keys,
               writes=[regions[0][1], regions[1][1]],
               tag=f"{tag}_mm{oc}_{pair}")
        for i in range(2):
            sink(oc, 2 * pair + i, regions[i][0], regions[i][1])


    def proj_chain(tag, oc, slot, rhs_of, rhs_keys):
        def _chain(te, oc=oc, slot=slot):
            last = None
            for dc in range(NC):
                rhs = rhs_of(dc)
                for tc in range(NTC):
                    pv, _ = psv(tc)
                    last = te.matmul(pv, wband[slot][:, 128 * dc:128 * dc + 128],
                                     rhs[:, 512 * tc:512 * tc + 512],
                                     start=(dc == 0), stop=(dc == NC - 1), **MM)
            return last
        P.step("pe", _chain, reads=[f"wband:{slot}"] + rhs_keys,
               writes=["PS0", "PS1", "PS2", "PS3"], tag=tag)

    def v_transpose_spill(tc, dst_s, oc, src_key, tag, sink_sb=None):
        def _tr(te, tc=tc):
            last = None
            for i in range(4):
                last = te.transpose(ptr[:, 128 * i:128 * i + 128],
                                    vstage[:, 512 * tc + 128 * i:
                                           512 * tc + 128 * i + 128], iden16)
            return last
        P.step("pe", _tr, reads=[src_key, "consts"],
               writes=["PTRmq"], tag=f"{tag}tr")
        if sink_sb is not None:
            def _tre(ve, tc=tc):
                last = None
                for i in range(4):
                    t = 4 * tc + i
                    last = ve.tensor_copy(sink_sb[:, 128 * t:128 * t + 128],
                                          ptr[:, 128 * i:128 * i + 128])
                return last
            P.step("dve", _tre, reads=["PTRmq"],
                   writes=[f"vtok:{tc}"], tag=f"{tag}te")
        else:
            def _tre(ve):
                last = None
                for i in range(4):
                    last = ve.tensor_copy(vtstage[:, 128 * i:128 * i + 128],
                                          ptr[:, 128 * i:128 * i + 128])
                return last
            P.step("dve", _tre, reads=["PTRmq"],
                   writes=["vt:stg"], tag=f"{tag}te")
            P.step("pool", (lambda gp, tc=tc, oc=oc: [
                gp.dma_start(dst_s[oc, 4 * tc + i],
                             vtstage[:, 128 * i:128 * i + 128])
                for i in range(4)]),
                dma=4, reads=["vt:stg"], writes=[f"v2_s:{oc}"], tag=f"{tag}sp")

    if phases >= 2:
        def p2_sink(oc, tc, pv, pk):
            alias = []
            if oc < 16:
                dst, dkey = q_slot(oc)[0], f"B{oc}:{tc}"
                if oc <= 5:
                    alias = ["mrows"]
                elif oc <= 7:
                    alias = ["mrows"] + [f"tmid:{t}" for t in range(NTC)]
                elif oc <= 9:
                    alias = ["lrow"]
                else:
                    alias = ["lr"]
            elif oc == 16:
                dst, dkey = k_sb, f"k_sb:{tc}"
            else:
                dst, dkey = vstage, f"vstage:{tc}"

            def _ev(sc, oc=oc, tc=tc, pv=pv, dst=dst):
                if oc < 16:
                    return sc.activation(dst[:, 512 * tc:512 * tc + 512],
                                         pv, AF.Identity,
                                         bias=sqb[:, oc:oc + 1], scale=SCALE)
                return sc.activation(dst[:, 512 * tc:512 * tc + 512], pv,
                                     AF.Identity,
                                     bias=vec_sb[:, V_CATTN + oc:
                                                 V_CATTN + oc + 1])
            P.step("act", _ev, reads=[pk, "consts", "sqb"],
                   writes=[dkey] + alias, tag=f"p2_ev{oc}_{tc}")
            if oc == 17:
                v_transpose_spill(tc, None, None, dkey, f"p2_v{tc}",
                                  sink_sb=v_tok)

        P2REG = [[(pscore[:, 0:512], "PS0"), (pscore[:, 512:1024], "PS1")],
                 [(pscore[:, 1024:1536], "PS2"), (pscore[:, 1536:2048], "PS3")]]
        for oc in [16, 17] + list(range(16)):
            slot = band_load(cattn, oc, True, f"p2_w{oc}")
            for pair in range(2):
                proj_pair("p2", oc, slot, lambda dc: xn_slot(dc)[0], AKEYS,
                          P2REG[pair], p2_sink, pair)

    if DBG.get("spy_qkv") and phases >= 2:
        spy1 = nc.dram_tensor("spy1", [18, 128, 2048], BF16,
                              kind="ExternalOutput")
        for oc in range(18):
            if oc < 16:
                srcb, keys = q_slot(oc)[0], [f"B{oc}:{t}" for t in range(NTC)]
            elif oc == 16:
                srcb, keys = k_sb, [f"k_sb:{t}" for t in range(NTC)]
            else:
                srcb, keys = v_tok, [f"vtok:{t}" for t in range(NTC)]
            P.step("pool", (lambda gp, oc=oc, srcb=srcb: gp.dma_start(
                spy1[oc], srcb)), dma=1, reads=keys, tag=f"spy1_{oc}")

    if phases >= 5:
        for tc in range(NTC):
            P.step("pe", (lambda te, tc=tc: te.matmul(
                putil[:, :], onesr16_at(0), r_rstd1[:, 512 * tc:512 * tc + 512],
                start=True, stop=True, **MM)),
                reads=["r_rstd1", "consts"], writes=["PUTIL"],
                tag=f"p3a_bc{tc}")
            P.step("dve", (lambda ve: ve.tensor_copy(rs_bc, putil[:, :])),
                   reads=["PUTIL"], writes=["rs_bc"], tag=f"p3a_bce{tc}")
            for dc in range(NC):
                P.step("dve", (lambda ve, dc=dc, tc=tc: ve.tensor_tensor(
                    s32[0], xn_slot(dc)[0][:, 512 * tc:512 * tc + 512], rs_bc,
                    ALU.mult)),
                    reads=[f"A{dc}:{tc}", "rs_bc"], writes=["s32:0"],
                    tag=f"p3a_m{dc}_{tc}")
                P.step("act", (lambda sc, dc=dc, tc=tc: sc.activation(
                    xn_slot(dc)[0][:, 512 * tc:512 * tc + 512], s32[0],
                    AF.Identity, bias=vec_sb[:, V_L1B + dc:V_L1B + dc + 1],
                    scale=vec_sb[:, V_L1S + dc:V_L1S + dc + 1])),
                    reads=["s32:0", "consts"], writes=[f"A{dc}:{tc}"],
                    tag=f"p3a_a{dc}_{tc}")

    Pbuf_mq = b16(OFF_C, 2048)
    PT_mq = b16(OFF_C + 9216, 2048)
    Pbuf_in = b16(OFF_C + 13312, 2048)
    PT_in = b16(OFF_C + 17408, 2048)
    k2h = [b16(OFF_H, 2048), b16(OFF_H + 4096, 2048)]
    v2h = [b16(OFF_H + 8192, 2048), b16(OFF_H + 12288, 2048)]
    q2h = [b16(OFF_G, 2048), b16(OFF_E + 9216, 2048)]
    ctstage = [b16(OFF_E + 8192 + 256 * i, 128) for i in range(2)]
    ptr_in = pacc2.bitcast(BF16)
    ptr_f32 = ptr.bitcast(F32)

    BUF_MQ = dict(P=Pbuf_mq, Pk="Pb_mq", PT=PT_mq, PTk="PTmq",
                  pacc=pacc, pacck="PACC", paccg="PACCg",
                  ptr=ptr, ptrk="PTRmq", pm=pmax_mq, pmk="pm_mq",
                  negm=small[:, 0:1], ell=small[:, 2:3], rell=small[:, 3:4],
                  nk="negm_mq", ek="ell_mq", rk="rell_mq")
    BUF_IN = dict(P=Pbuf_in, Pk="Pb_in", PT=PT_in, PTk="PTin",
                  pacc=putil, pacck="PUTIL", paccg="PUTILg",
                  ptr=ptr_in, ptrk="PTRin", pm=pmax_in, pmk="pm_in",
                  negm=small[:, 8:9], ell=small[:, 10:11],
                  rell=small[:, 11:12],
                  nk="negm_in", ek="ell_in", rk="rell_in")

    def emit_attn_tile(h, j, q_ap, qcol, q_keys, k_ap, k_keys, v_ap, v_keys,
                       sink, tag, bufs, extra_first=()):
        K = 128 * (j + 1)
        nkc = (K + 511) // 512
        pskeys = [f"PS{i}" for i in range(nkc)]
        Pb, Pk = bufs["P"], bufs["Pk"]
        PTb, PTk = bufs["PT"], bufs["PTk"]
        exp_alias = []
        pacc_b, pacck, paccg = bufs["pacc"], bufs["pacck"], bufs["paccg"]
        ptr_b, ptrk = bufs["ptr"], bufs["ptrk"]
        negm_c, ell_c, rell_c = bufs["negm"], bufs["ell"], bufs["rell"]
        nk, ek, rk = bufs["nk"], bufs["ek"], bufs["rk"]

        pm, pmk = bufs["pm"], bufs["pmk"]
        if nkc == 1:
            def _sc(te):
                return te.matmul(pscore[:, 0:K], q_ap[:, qcol:qcol + 128],
                                 k_ap[:, 0:K], start=True, stop=True, **MM)
            P.step("pe", _sc, reads=q_keys + k_keys + ["consts"],
                   writes=pskeys + list(extra_first), tag=f"{tag}sc{h}_{j}")

            def _msk(ve):
                ve.tensor_tensor(pscore[:, 128 * j:128 * j + 128],
                                 pscore[:, 128 * j:128 * j + 128], tri32,
                                 ALU.add)
                return ve.reduce_max(negm_c, pscore[:, 0:K], AX.X, negate=True)
            P.step("dve", _msk, reads=pskeys, writes=["PS0", nk],
                   tag=f"{tag}mx{h}_{j}")
        else:
            for kc in range(nkc):
                n = min(512, K - 512 * kc)
                lastc = kc == nkc - 1

                def _sck(te, kc=kc, n=n):
                    return te.matmul(pscore[:, 512 * kc:512 * kc + n],
                                     q_ap[:, qcol:qcol + 128],
                                     k_ap[:, 512 * kc:512 * kc + n],
                                     start=True, stop=True, **MM)
                P.step("pe", _sck, reads=q_keys + k_keys + ["consts"],
                       writes=[f"PS{kc}"] +
                              (list(extra_first) if kc == 0 else []),
                       tag=f"{tag}sc{h}_{j}_{kc}")

                def _pmx(ve, kc=kc, n=n, lastc=lastc):
                    if lastc:
                        off = 128 * j - 512 * kc
                        ve.tensor_tensor(pscore[:, 128 * j:128 * j + 128],
                                         pscore[:, 128 * j:128 * j + 128],
                                         tri32, ALU.add)
                    return ve.reduce_max(pm[:, kc:kc + 1],
                                         pscore[:, 512 * kc:512 * kc + n],
                                         AX.X)
                P.step("dve", _pmx, reads=[f"PS{kc}"],
                       writes=[pmk] + ([f"PS{kc}"] if lastc else []),
                       tag=f"{tag}pm{h}_{j}_{kc}")

            def _cmb(ve):
                ve.tensor_copy(junkb, iden16)
                return ve.reduce_max(negm_c, pm[:, 0:nkc], AX.X, negate=True)
            P.step("dve", _cmb, reads=[pmk], writes=[nk],
                   tag=f"{tag}cmb{h}_{j}")

        def _exp(sc):
            sc.memzero(ell_c)
            return sc.activation(Pb[:, 0:K], pscore[:, 0:K], AF.Exp,
                                 bias=negm_c, accum_out=ell_c)
        P.step("act", _exp, reads=pskeys + [nk],
               writes=[Pk, ek] + exp_alias, tag=f"{tag}ex{h}_{j}")

        def _nrm(ve):
            ve.reciprocal(rell_c, ell_c)
            ve.tensor_copy(junkb, iden16)
            ve.tensor_copy(junkb, iden16)
            return ve.tensor_scalar(Pb[:, 0:K], Pb[:, 0:K], rell_c, None,
                                    ALU.mult)
        P.step("dve", _nrm, reads=[Pk, ek], writes=[Pk, rk],
               tag=f"{tag}nr{h}_{j}")

        def back():
            ngrp = (j + 8) // 8
            for g in range(ngrp):
                k0 = 8 * g
                n = min(8, j + 1 - k0)

                def _tr(te, k0=k0, n=n):
                    last = None
                    for i in range(n):
                        last = te.transpose(ptr_b[:, 128 * i:128 * i + 128],
                                            Pb[:, 128 * (k0 + i):
                                               128 * (k0 + i) + 128], iden16)
                    return last
                P.step("pe", _tr, reads=[Pk, "consts"], writes=[ptrk],
                       tag=f"{tag}tr{h}_{j}_{g}")

                def _tre(ve, k0=k0, n=n):
                    return ve.tensor_copy(PTb[:, 128 * k0:128 * (k0 + n)],
                                          ptr_b[:, 0:128 * n])
                P.step("dve", _tre, reads=[ptrk], writes=[PTk],
                       tag=f"{tag}te{h}_{j}_{g}")

                def _pv(te, k0=k0, n=n):
                    last = None
                    for i in range(n):
                        kt = k0 + i
                        last = te.matmul(pacc_b[:, 0:128],
                                         v_ap[:, 128 * kt:128 * kt + 128],
                                         PTb[:, 128 * kt:128 * kt + 128],
                                         start=(kt == 0), stop=(kt == j), **MM)
                    return last
                P.step("pe", _pv, reads=[PTk, paccg] + v_keys,
                       writes=([pacck] if (g == 0 or g == ngrp - 1) else []),
                       tag=f"{tag}pv{h}_{j}_{g}")
            sink(h, j)
        return back

    cast_pieces = []
    if phases >= 8:
        for nm, src_t, dst, rows_per in (("wo", wo_i, wo_img, 4),
                                         ("ff1", ff1_i, ff1_img, 4),
                                         ("ff2", ff2_i, ff2_img, 1)):
            n0 = src_t.shape[0]
            for pi in range(n0 // rows_per):
                def mkc(nm=nm, src_t=src_t, dst=dst, pi=pi, rows_per=rows_per):
                    def emitc():
                        P.step("pool", (lambda gp: gp.dma_start(
                            dst[pi * rows_per:(pi + 1) * rows_per],
                            src_t[pi * rows_per:(pi + 1) * rows_per])),
                            dma=1, writes=[f"{nm}img:{pi}"],
                            tag=f"cast_{nm}{pi}")
                    return emitc
                cast_pieces.append(mkc())

    def emit_casts(n):
        for _ in range(n):
            if cast_pieces:
                cast_pieces.pop(0)()

    bandctr = [100]

    def next_band(img, cast, tag, reads=(), extra_writes=()):
        slot = bandctr[0] % 3
        bandctr[0] += 1
        eng = "pool" if cast else "sp"
        P.step(eng, (lambda e, img=img, slot=slot: e.dma_start(
            wband[slot], img)),
            dma=1, reads=list(reads),
            writes=[f"wband:{slot}"] + list(extra_writes), tag=tag)
        return slot

    if phases >= 3:
        def mqa_sink(h, j):
            def _ev(sc, h=h, j=j):
                dst, _ = q_slot(h)
                return sc.activation(dst[:, 128 * j:128 * j + 128],
                                     pacc[:, 0:128], AF.Identity)
            P.step("act", _ev, reads=["PACC"],
                   writes=[f"ctx:{h}:{j}", "PACCg"], tag=f"mqev{h}_{j}")

        def emit_mq(h, j):
            q_ap, _ = q_slot(h)
            return emit_attn_tile(h, j, q_ap, 128 * j, [f"B{h}:{j // 4}"],
                                  k_sb, [f"k_sb:{i}" for i in range(NTC)],
                                  v_tok, [f"vtok:{i}" for i in range(NTC)],
                                  mqa_sink, "mq", BUF_MQ)

        def q2_sink(oc, tc, pv, pk):
            P.step("act", (lambda sc, tc=tc, pv=pv: sc.activation(
                vstage[:, 512 * tc:512 * tc + 512], pv, AF.Identity,
                scale=SCALE)),
                reads=[pk], writes=[f"vstage:{tc}"], tag=f"p3q2ev{oc}_{tc}")
            P.step("pool", (lambda gp, oc=oc, tc=tc: gp.dma_start(
                q2_s[oc, tc], vstage[:, 512 * tc:512 * tc + 512])),
                dma=1, reads=[f"vstage:{tc}"], writes=[f"q2_s:{oc}"],
                tag=f"p3q2sp{oc}_{tc}")

        def k2_sink(oc, tc, pv, pk):
            P.step("act", (lambda sc, tc=tc, pv=pv: sc.activation(
                vstage[:, 512 * tc:512 * tc + 512], pv, AF.Identity)),
                reads=[pk], writes=[f"vstage:{tc}"], tag=f"p3k2ev{oc}_{tc}")
            P.step("pool", (lambda gp, oc=oc, tc=tc: gp.dma_start(
                k2_s[oc, tc], vstage[:, 512 * tc:512 * tc + 512])),
                dma=1, reads=[f"vstage:{tc}"], writes=[f"k2_s:{oc}"],
                tag=f"p3k2sp{oc}_{tc}")

        def v2_sink(oc, tc, pv, pk):
            P.step("act", (lambda sc, tc=tc, pv=pv: sc.activation(
                vstage[:, 512 * tc:512 * tc + 512], pv, AF.Identity)),
                reads=[pk], writes=[f"vstage:{tc}"], tag=f"p3v2ev{oc}_{tc}")
            v_transpose_spill(tc, v2_s, oc, f"vstage:{tc}", f"p3v2t{oc}_{tc}")

        CTXKEYS = [f"ctx:{d}:{jj}" for d in range(NC) for jj in range(NC)]
        REG_B = [(putil[:, :], "PUTIL"), (pacc2[:, :], "PACC2")]
        REG_C = [(pacc[:, :], "PACC"), (ptr_f32[:, 0:512], "PTRmq")]

        def A_sink(oc, tc, pv, pk):
            P.step("act", (lambda sc, oc=oc, tc=tc, pv=pv: sc.activation(
                vstage[:, 512 * tc:512 * tc + 512], pv, AF.Identity,
                bias=vec_sb[:, V_CPROJ + oc:V_CPROJ + oc + 1])),
                reads=[pk, "consts"],
                writes=[f"vstage:{tc}"], tag=f"p2c_ev{oc}_{tc}")
            P.step("act", (lambda sc, oc=oc, tc=tc: sc.dma_start(
                A_s[oc, tc], vstage[:, 512 * tc:512 * tc + 512])),
                dma=1, reads=[f"vstage:{tc}"], writes=[f"A_s:{oc}:{tc}"],
                tag=f"p2c_sp{oc}_{tc}")

        def inner_sink(h, j):
            sl2 = (h * NC + j) % 2

            def _ev(ve, sl2=sl2):
                return ve.tensor_copy(ctstage[sl2], putil[:, 0:128])
            P.step("dve", _ev, reads=["PUTIL"],
                   writes=[f"ct:stg{sl2}", "PUTILg"] +
                          (["vt:stg"] if h == 0 and j <= 1 else []),
                   tag=f"p4ev{h}_{j}")
            P.step("act", (lambda sc, h=h, j=j, sl2=sl2: sc.dma_start(
                ctx2_s[h, j], ctstage[sl2])),
                dma=1, reads=[f"ct:stg{sl2}"], writes=[f"ctx2_s:{h}:{j}"],
                tag=f"p4sp{h}_{j}")

        def emit_in_loads(h):
            sl = h % 2
            P.step("sp", (lambda sy, h=h, sl=sl: sy.dma_start(
                k2h[sl].rearrange("p (b f) -> p b f", b=4),
                k2_s.rearrange("a b p f -> a p b f")[h])),
                dma=1, reads=[f"k2_s:{h}"],
                writes=[f"k2h:{sl}"] +
                       ((["r_mu"] if sl == 0 else ["r_var"]) if h < 2 else []),
                tag=f"p4_ldk{h}")
            P.step("sp", (lambda sy, h=h, sl=sl: sy.dma_start(
                v2h[sl].rearrange("p (b f) -> p b f", b=16),
                v2_s.rearrange("a b p f -> a p b f")[h])),
                dma=1, reads=[f"v2_s:{h}"],
                writes=[f"v2h:{sl}"] +
                       ((["r_rstd"] if sl == 0 else ["r_rstd1", "r_mu2",
                                                     "r_rs2"]) if h < 2
                        else []),
                tag=f"p4_ldv{h}")
            P.step("sp", (lambda sy, h=h, sl=sl: sy.dma_start(
                q2h[sl].rearrange("p (b f) -> p b f", b=4),
                q2_s.rearrange("a b p f -> a p b f")[h])),
                dma=1, reads=[f"q2_s:{h}"],
                writes=[f"q2h:{sl}"] +
                       ((["mu_bc", "rs_bc"] if sl == 0 else
                         ["xnt:0", "xnt:1", "At:0", "At:1"]) if h < 2
                        else []),
                tag=f"p4_ldq{h}")

        def emit_in(h, j):
            sl = h % 2
            return emit_attn_tile(h, j, q2h[sl], 128 * j, [f"q2h:{sl}"],
                                  k2h[sl], [f"k2h:{sl}"], v2h[sl],
                                  [f"v2h:{sl}"], inner_sink, "in", BUF_IN)

        do_inner = phases >= 7
        do_p3b = phases >= 6
        do_p2c = phases >= 4
        FILL_AT = {2: 0, 5: 1, 7: 2, 9: 3, 11: 4, 13: 5}
        pend = {"mq": None, "in": None}

        def drain(key):
            if pend[key] is not None:
                pend[key]()
                pend[key] = None

        for h in range(H):
            units = []
            if do_p3b:
                for which, img, sink in (("q2", wq_i, q2_sink),
                                         ("k2", wk_i, k2_sink),
                                         ("v2", wv_i, v2_sink)):
                    def mk_unit(which=which, img=img, sink=sink, h=h):
                        state = {}

                        def u0():
                            state["slot"] = next_band(
                                img.rearrange("o p c f -> o p (c f)")[h],
                                True, f"p3{which}w{h}")
                            proj_pair(f"p3{which}", h, state["slot"],
                                      lambda dc: xn_slot(dc)[0], AKEYS,
                                      REG_B, sink, 0)

                        def u1():
                            proj_pair(f"p3{which}", h, state["slot"],
                                      lambda dc: xn_slot(dc)[0], AKEYS,
                                      REG_B, sink, 1)
                        return [u0, u1]
                    units.extend(mk_unit())
            for j in range(NC):
                drain("mq")
                b = emit_mq(h, j)
                pend["mq"] = b
                if do_inner and h >= 1:
                    drain("in")
                    b2 = emit_in(h - 1, j)
                    pend["in"] = b2
                ui = FILL_AT.get(j)
                if ui is not None and ui < len(units):
                    units[ui]()
                if j == 14 and do_inner and do_p3b:
                    emit_in_loads(h)
            for u in units[6:]:
                u()
            emit_casts(2)
        drain("mq")

        cstate = {}

        def c_u(oc, pair):
            if pair == 0:
                cstate[oc] = next_band(
                    cproj.rearrange("o p c f -> o p (c f)")[oc], True,
                    f"p2c_w{oc}")
            proj_pair("p2c", oc, cstate[oc], lambda dc: q_slot(dc)[0],
                      CTXKEYS, REG_C, A_sink, pair)

        cu_list = []
        if do_p2c:
            cu_list = [(oc, pr) for oc in range(NC) for pr in (0, 1)]
        ci = 0
        if do_inner:
            for j in range(NC):
                drain("in")
                b2 = emit_in(15, j)
                pend["in"] = b2
                for _ in range(2):
                    if ci < len(cu_list):
                        c_u(*cu_list[ci])
                        ci += 1
            drain("in")
        while ci < len(cu_list):
            c_u(*cu_list[ci])
            ci += 1

    if DBG.get("spy_ctx") and phases >= 3:
        spy2 = nc.dram_tensor("spy2", [16, 128, 2048], BF16,
                              kind="ExternalOutput")
        for oc in range(16):
            P.step("pool", (lambda gp, oc=oc: gp.dma_start(
                spy2[oc], q_slot(oc)[0])),
                dma=1, reads=[f"ctx:{oc}:{j}" for j in range(16)],
                tag=f"spy2_{oc}")

    if phases >= 8:
        emit_casts(len(cast_pieces))

    if phases >= 8:
        tT_sb = b16(OFF_C, 2048, rows=RLOW)
        wup_sb = b16(OFF_C + 4096, 2048, rows=RLOW)
        P.step("sp", (lambda sy: sy.dma_start(tT_sb, t_s[:, :])),
               dma=1, reads=["t_s"], writes=["tTsb", "Pb_mq"], tag="t_tld")
        P.step("sp", (lambda sy: sy.dma_start(
            wup_sb, wupb_s.rearrange("r c f -> r (c f)"))),
            dma=1, reads=["wup_s"],
            writes=["wupsb"] +
                   [f"vstage:{t}" for t in range(NTC)], tag="t_wupld")

        def x2_slot(dc):
            return b16(OFF_B + 1024 * dc, 512), f"x2:{dc}"

        def xn2p_slot(dc):
            return b16(OFF_B + 16384 + 1024 * dc, 512), f"xn2p:{dc}"

        def h_slot(fc):
            return b16(OFF_A + 1024 * fc, 512), f"h:{fc}"

        def c2b_slot(dc):
            return b16(OFF_B + 36864 + 1024 * dc, 512), f"c2b:{dc}"

        C2OVER = [f"ctx:{d}:{jj}" for d in (9, 10, 11, 12) for jj in range(NC)]
        sqstage = b16(OFF_F, 512)
        xnt = [b16(OFF_E + 9216 + 1024 * i, 512) for i in range(2)]
        At = [b16(OFF_E + 11264 + 1024 * i, 512) for i in range(2)]
        EOVER = ["ct:stg0", "ct:stg1", "vt:stg"]
        mrow = f32v(OFF_H, 512, rows=1, p0=0)
        m0b = b16(OFF_B + 32768, 512)
        m1b = b16(OFF_B + 33792, 512)
        m2b = b16(OFF_B + 34816, 512)
        MBS = [m0b, m1b, m2b]

        def tail_band(img, idx, tag, imgkey):
            slot = idx % 3
            P.step("sp", (lambda sy, img=img, idx=idx, slot=slot: sy.dma_start(
                wband[slot], img.rearrange("o p c f -> o p (c f)")[idx])),
                dma=1, reads=[imgkey], writes=[f"wband:{slot}"], tag=tag)
            return slot

        for bt in range(NB):
            for mi in range(3):
                P.step("sp", (lambda sy, mi=mi, bt=bt: sy.dma_start(
                    mrow, mrow_s[mi:mi + 1, 512 * bt:512 * bt + 512])),
                    dma=1, reads=["mrow_s"], writes=["mrowt"],
                    tag=f"t{bt}_mld{mi}")
                P.step("pe", (lambda te: te.matmul(
                    putil[:, :], onesr_at(0), mrow, start=True, stop=True, **MM)),
                    reads=["mrowt", "consts"], writes=["PUTIL"],
                    tag=f"t{bt}_mbc{mi}")
                P.step("dve", (lambda ve, mi=mi: ve.tensor_copy(
                    MBS[mi], putil[:, :])),
                    reads=["PUTIL"],
                    writes=[f"mb:{mi}"], tag=f"t{bt}_mbe{mi}")

            for dc in range(NC):
                c2, c2k = c2b_slot(dc)
                P.step("sp", (lambda sy, dc=dc, bt=bt, c2=c2: [
                    sy.dma_start(c2[:, 128 * i:128 * i + 128],
                                 ctx2_s[dc, 4 * bt + i]) for i in range(4)]),
                    dma=4,
                    reads=[f"ctx2_s:{dc}:{4 * bt + i}" for i in range(4)],
                    writes=[c2k] + (C2OVER if bt == 0 else []),
                    tag=f"t{bt}_c2ld{dc}")

            for oc in range(NC):
                slot = tail_band(wo_img, oc, f"t{bt}_wow{oc}",
                                 f"woimg:{oc // 4}")

                def _chain(te, slot=slot):
                    last = None
                    for dc in range(NC):
                        last = te.matmul(pacc2[:, :],
                                         wband[slot][:, 128 * dc:128 * dc + 128],
                                         c2b_slot(dc)[0],
                                         start=(dc == 0), stop=(dc == NC - 1),
                                         **MM)
                    return last
                P.step("pe", _chain,
                       reads=[f"wband:{slot}"] +
                             [c2b_slot(d)[1] for d in range(NC)],
                       writes=["PACC2"], tag=f"t{bt}_womm{oc}")
                xsl = oc % 2
                P.step("sp", (lambda sy, oc=oc, bt=bt, xsl=xsl: sy.dma_start(
                    xnt[xsl], xn_s[oc, bt])),
                    dma=1, reads=[f"xn_s:{oc}"],
                    writes=[f"xnt:{xsl}"] + EOVER +
                           (["q2h:1"] if bt == 0 and oc < 2 else []),
                    tag=f"t{bt}_xnld{oc}")

                def _x2(ve, oc=oc, xsl=xsl):
                    return ve.tensor_tensor(x2_slot(oc)[0], pacc2[:, :],
                                            xnt[xsl], ALU.add)
                P.step("dve", _x2, reads=["PACC2", f"xnt:{xsl}"],
                       writes=[x2_slot(oc)[1]], tag=f"t{bt}_x2{oc}")
                P.step("act", (lambda sc, oc=oc: sc.activation(
                    sqstage, x2_slot(oc)[0], AF.Square)),
                    reads=[x2_slot(oc)[1]],
                    writes=["sqst"] + (["k_sb:0"] if bt == 0 and oc == 0
                                       else []),
                    tag=f"t{bt}_sq{oc}")

                def _st(te, oc=oc):
                    te.matmul(putil[0:1, :], ones16_col, x2_slot(oc)[0],
                              start=(oc == 0), stop=(oc == NC - 1), **MM)
                    return te.matmul(pscore[0:1, 0:512], ones16_col,
                                     sqstage,
                                     start=(oc == 0), stop=(oc == NC - 1), **MM)
                P.step("pe", _st,
                       reads=[x2_slot(oc)[1], "sqst", "consts"],
                       writes=(["PUTIL", "PS0"] if oc in (0, NC - 1) else []),
                       tag=f"t{bt}_st{oc}")

            def _l2r(ve):
                ve.tensor_scalar(r_mu2, putil[0:1, :], inv_nr, None,
                                 ALU.mult)
                ve.tensor_scalar(r_rs2, pscore[0:1, 0:512], inv_nr, None,
                                 ALU.mult)
                ve.tensor_tensor(s32[0][0:1, :], r_mu2, r_mu2, ALU.mult)
                ve.tensor_copy(junkb, iden16)
                ve.tensor_copy(junkb, iden16)
                return ve.tensor_tensor(r_rs2, r_rs2, s32[0][0:1, :],
                                        ALU.subtract)
            P.step("dve", _l2r, reads=["PUTIL", "PS0", "epsc"],
                   writes=["r_mu2", "r_rs2", "s32:0"] +
                          (["k2h:0"] if bt == 0 else []),
                   tag=f"t{bt}_l2r")

            def _l2s(sc):
                sc.activation(s32[0][0:1, :], r_rs2, AF.Ln, bias=epsr)
                sc.copy(junkb, iden16)
                sc.copy(junkb, iden16)
                return sc.activation(r_rs2, s32[0][0:1, :], AF.Exp, scale=-0.5)
            P.step("act", _l2s, reads=["r_rs2", "epsc"],
                   writes=["r_rs2", "s32:0"], tag=f"t{bt}_l2s")

            def _l2b(te):
                te.matmul(putil[:, :], onesr_at(0), r_mu2, start=True,
                          stop=True, **MM)
                return te.matmul(pacc2[:, :], onesr_at(0), r_rs2, start=True,
                                 stop=True, **MM)
            P.step("pe", _l2b, reads=["r_mu2", "r_rs2", "consts"],
                   writes=["PUTIL", "PACC2"], tag=f"t{bt}_l2b")

            def _l2be(ve):
                ve.tensor_copy(mu_bc, putil[:, :])
                return ve.tensor_copy(rs_bc, pacc2[:, :])
            P.step("dve", _l2be, reads=["PUTIL", "PACC2"],
                   writes=["mu_bc", "rs_bc"] +
                          (["q2h:0"] if bt == 0 else []), tag=f"t{bt}_l2be")

            for dc in range(NC):
                def _xa(ve, dc=dc):
                    ve.tensor_tensor(s32[0], x2_slot(dc)[0], mu_bc,
                                     ALU.subtract)
                    return ve.tensor_tensor(s32[0], s32[0], rs_bc, ALU.mult)
                P.step("dve", _xa, reads=[x2_slot(dc)[1], "mu_bc", "rs_bc"],
                       writes=["s32:0"], tag=f"t{bt}_xa{dc}")
                P.step("act", (lambda sc, dc=dc: sc.activation(
                    xn2p_slot(dc)[0], s32[0], AF.Identity,
                    bias=vec_sb[:, V_L2B + dc:V_L2B + dc + 1],
                    scale=vec_sb[:, V_L2S + dc:V_L2S + dc + 1])),
                    reads=["s32:0", "consts"], writes=[xn2p_slot(dc)[1]],
                    tag=f"t{bt}_xb{dc}")

            XN2PK = [xn2p_slot(d)[1] for d in range(NC)]
            for fc in range(FFC):
                slot = tail_band(ff1_img, fc, f"t{bt}_f1w{fc}",
                                 f"ff1img:{fc // 4}")
                pv, pvk = psv(fc % 3)

                def _chain(te, slot=slot, pv=pv):
                    last = None
                    for dc in range(NC):
                        last = te.matmul(pv,
                                         wband[slot][:, 128 * dc:128 * dc + 128],
                                         xn2p_slot(dc)[0],
                                         start=(dc == 0), stop=(dc == NC - 1),
                                         **MM)
                    return last
                P.step("pe", _chain, reads=[f"wband:{slot}"] + XN2PK,
                       writes=[pvk], tag=f"t{bt}_f1mm{fc}")
                P.step("act", (lambda sc, fc=fc, pv=pv: sc.activation(
                    h_slot(fc)[0], pv, AF.Gelu_apprx_tanh)),
                    reads=[pvk], writes=[h_slot(fc)[1]], tag=f"t{bt}_g{fc}")

            for dc in range(NC):
                pv, pvk = psv(dc % 3)
                for qb in range(4):
                    slotq = qb % 2
                    P.step("sp", (lambda sy, dc=dc, qb=qb, slotq=slotq:
                                  sy.dma_start(
                        wband[slotq],
                        ff2_img.rearrange("o p c f -> o p (c f)")[dc]
                        [:, 2048 * qb:2048 * qb + 2048])),
                        dma=1, reads=[f"ff2img:{dc}"],
                        writes=[f"wband:{slotq}"], tag=f"t{bt}_f2w{dc}_{qb}")

                    def _chq(te, qb=qb, slotq=slotq, pv=pv):
                        last = None
                        for fi in range(16):
                            fc = 16 * qb + fi
                            last = te.matmul(
                                pv, wband[slotq][:, 128 * fi:128 * fi + 128],
                                h_slot(fc)[0], start=(fc == 0),
                                stop=(fc == FFC - 1), **MM)
                        return last
                    P.step("pe", _chq,
                           reads=[f"wband:{slotq}"] +
                                 [h_slot(16 * qb + i)[1] for i in range(16)],
                           writes=([pvk] if qb in (0, 3) else []),
                           tag=f"t{bt}_f2mm{dc}_{qb}")

                P.step("pe", (lambda te, dc=dc, bt=bt: te.matmul(
                    pacc2[:, :], wup_sb[:, 128 * dc:128 * dc + 128],
                    tT_sb[:, 512 * bt:512 * bt + 512], start=True, stop=True,
                    **MM)),
                    reads=["wupsb", "tTsb"], writes=["PACC2"],
                    tag=f"t{bt}_low{dc}")

                xsl = dc % 2
                P.step("sp", (lambda sy, dc=dc, bt=bt, xsl=xsl: sy.dma_start(
                    xnt[xsl], xn_s[dc, bt])),
                    dma=1, reads=[f"xn_s:{dc}"], writes=[f"xnt:{xsl}"] + EOVER,
                    tag=f"t{bt}_axn{dc}")
                P.step("sp", (lambda sy, dc=dc, bt=bt, xsl=xsl: sy.dma_start(
                    At[xsl], A_s[dc, bt])),
                    dma=1, reads=[f"A_s:{dc}:{bt}"], writes=[f"At:{xsl}"],
                    tag=f"t{bt}_aA{dc}")
                xbslot = f32v(OFF_E + 4096 + 2048 * xsl, 512)
                P.step("sp", (lambda sy, dc=dc, bt=bt, xbslot=xbslot:
                              sy.dma_start(xbslot, xT[dc, bt])),
                    dma=1, writes=[f"xb:{xsl}"], tag=f"t{bt}_ax{dc}")

                def _asm(ve, dc=dc, xsl=xsl, pv=pv, xbslot=xbslot):
                    o = s32[1]
                    t = s32[0]
                    ve.tensor_tensor(t, pv, x2_slot(dc)[0], ALU.add)
                    ve.tensor_tensor(o, t, m2b, ALU.mult)
                    ve.tensor_tensor(t, pacc2[:, :], xnt[xsl], ALU.add)
                    ve.tensor_tensor(t, t, m1b, ALU.mult)
                    ve.tensor_tensor(o, o, t, ALU.add)
                    ve.tensor_scalar(t, xnt[xsl],
                                     vec_sb[:, V_SB + dc:V_SB + dc + 1], None,
                                     ALU.add)
                    ve.tensor_tensor(t, t, m0b, ALU.mult)
                    ve.tensor_tensor(o, o, t, ALU.add)
                    ve.tensor_tensor(t, xbslot, At[xsl], ALU.add)
                    return ve.tensor_tensor(o, t, o, ALU.add)
                P.step("dve", _asm,
                       reads=[pvk, x2_slot(dc)[1], "PACC2", f"xnt:{xsl}",
                              f"At:{xsl}", f"xb:{xsl}", "mb:0", "mb:1",
                              "mb:2", "consts"],
                       writes=["s32:1", "s32:0"], tag=f"t{bt}_asm{dc}")
                P.step("pool", (lambda gp, dc=dc, bt=bt: gp.dma_start(
                    outT[dc, bt], s32[1])),
                    dma=1, reads=["s32:1"], writes=["outw"],
                    tag=f"t{bt}_out{dc}")

    stack = contextlib.ExitStack()
    sems = {}
    for e in Prog.ENG:
        sems[("comp", e)] = stack.enter_context(nc.semaphore(f"c_{e}"))
        for i in range(Prog.POOLN[e]):
            if (e, i) in P.pool_uses or True:
                sems[("dmap", e, i)] = stack.enter_context(
                    nc.semaphore(f"d_{e}{i}"))
    waited = {e: {} for e in Prog.ENG}
    with stack:
        with nc.Block() as block:
            secmap = {"pe": block.tensor, "dve": block.vector,
                      "act": block.scalar, "pool": block.gpsimd,
                      "sp": block.sync}

            def mk(e):
                def _run(engobj, e=e):
                    _emit_engine(P, e, engobj, sems, waited)
                    for (ee, i), u in sorted(P.pool_uses.items()):
                        if ee == e:
                            engobj.wait_ge(sems[("dmap", ee, i)], 16 * u)
                return _run
            for e in ["sp", "pool", "pe", "dve", "act"]:
                secmap[e](mk(e))
    return nc


def make_inputs(L0_attn_hidden_states, L0_attn_c_attn_w, L0_attn_c_attn_b,
                L0_attn_c_proj_w, L0_attn_c_proj_b, L0_ffn_simple_bias,
                L0_ffn_w_down, L0_ffn_w_up, L0_ffn_w_router, L0_ffn_ln1_scale,
                L0_ffn_ln1_bias, L0_ffn_wq, L0_ffn_wk, L0_ffn_wv, L0_ffn_wo,
                L0_ffn_ln2_scale, L0_ffn_ln2_bias, L0_ffn_w_ff1, L0_ffn_w_ff2):
    f = lambda a: np.asarray(a, dtype=np.float32)
    x = f(L0_attn_hidden_states)
    vecs = np.concatenate([
        pack_vec(f(L0_attn_c_attn_b), 18), pack_vec(f(L0_attn_c_proj_b), 16),
        pack_vec(f(L0_ffn_simple_bias), 16), pack_vec(f(L0_ffn_ln1_scale), 16),
        pack_vec(f(L0_ffn_ln1_bias), 16), pack_vec(f(L0_ffn_ln2_scale), 16),
        pack_vec(f(L0_ffn_ln2_bias), 16)], axis=1)
    shared = {
        "cattn": blk_lhsT(f(L0_attn_c_attn_w)),
        "cproj": blk_lhsT(f(L0_attn_c_proj_w)),
        "wq": blk_lhsT(f(L0_ffn_wq)),
        "wk": blk_lhsT(f(L0_ffn_wk)),
        "wv": blk_lhsT(f(L0_ffn_wv)),
        "wo": blk_lhsT(f(L0_ffn_wo)),
        "ff1": blk_lhsT(f(L0_ffn_w_ff1)),
        "ff2": blk_ff2(f(L0_ffn_w_ff2)),
        "wdn": np.ascontiguousarray(
            f(L0_ffn_w_down).T.reshape(16, 128, RLOW).transpose(1, 0, 2)),
        "wup": np.ascontiguousarray(f(L0_ffn_w_up).T.reshape(RLOW, 16, 128)),
        "wrt": np.ascontiguousarray(
            f(L0_ffn_w_router).T.reshape(16, 128, 3).transpose(1, 0, 2)),
        "vecs": vecs,
    }
    return [dict(shared, xT=blk_x(x[b])) for b in range(B)]


def kernel(**inputs):
    in_maps = make_inputs(**inputs)
    nc = build_program()
    res = run_bass_kernel_spmd(nc, in_maps, core_ids=list(range(B)))
    DBG["res"] = res
    out = np.stack([unblk_out(r["outT"]) for r in res.results])
    return out.astype(np.float32)


# version 10
# speedup vs baseline: 1.1803x; 1.1803x over previous
import contextlib
import numpy as np
import ml_dtypes
import concourse.bass as bass
import concourse.mybir as mybir
from concourse.bass_utils import run_bass_kernel_spmd

F32 = mybir.dt.float32
BF16 = mybir.dt.bfloat16
AF = mybir.ActivationFunctionType
ALU = mybir.AluOpType
AX = mybir.AxisListType

B, S, D = 8, 2048, 2048
NC = 16
NTC = 4
H = 16
FF = 8192
FFC = 64
RLOW = 32
NB = 4
SCALE = float(1.0 / np.sqrt(128.0))
EPS = 1e-5
NEG = -1.0e30

DBG = {"phases": 99, "outs": ()}

OFF_A = 0
OFF_B = 65536
OFF_C = 131072
OFF_D = 152576
OFF_E = 164864
OFF_F = 178176
OFF_G = 186368
OFF_H = 190464
OFF_I = 206848


class Prog:
    ENG = ["pe", "dve", "act", "pool", "sp"]
    POOLN = {"sp": 20, "pool": 20, "act": 12, "dve": 2, "pe": 2}

    def __init__(self):
        self.steps = {e: [] for e in self.ENG}
        self.last_writer = {}
        self.readers = {}
        self.dma_count = {e: 0 for e in self.ENG}
        self.comp_count = {e: 0 for e in self.ENG}
        self.pool_uses = {}

    def step(self, eng, emit, reads=(), writes=(), deps=(), dma=0, tag=""):
        st = {"eng": eng, "emit": emit, "dma": dma, "tag": tag, "waits": {}}
        if dma:
            idx = self.dma_count[eng] % self.POOLN[eng]
            self.dma_count[eng] += 1
            u = self.pool_uses.get((eng, idx), 0) + dma
            self.pool_uses[(eng, idx)] = u
            st["dma_sem"] = (eng, idx)
            st["dma_post"] = u
        else:
            self.comp_count[eng] += 1
            st["comp_post"] = self.comp_count[eng]
        depset = list(deps)
        for k in reads:
            w = self.last_writer.get(k)
            if w is not None:
                depset.append(w)
        for k in writes:
            w = self.last_writer.get(k)
            if w is not None:
                depset.append(w)
            depset.extend(self.readers.get(k, ()))
        for d in depset:
            if d is None or d is st:
                continue
            if d["eng"] == eng and not d["dma"]:
                continue
            if d["dma"]:
                sem, val = ("dmap",) + d["dma_sem"], 16 * d["dma_post"]
            else:
                sem, val = ("comp", d["eng"]), d["comp_post"]
            if st["waits"].get(sem, 0) < val:
                st["waits"][sem] = val
        for k in reads:
            self.readers.setdefault(k, []).append(st)
        for k in writes:
            self.last_writer[k] = st
            self.readers[k] = []
        self.steps[eng].append(st)
        return st


def _emit_engine(prog, eng_name, engobj, sems, waited):
    for st in prog.steps[eng_name]:
        for sem_key in sorted(st["waits"], key=str):
            val = st["waits"][sem_key]
            if waited[eng_name].get(sem_key, 0) >= val:
                continue
            engobj.wait_ge(sems[sem_key], val)
            waited[eng_name][sem_key] = val
        out = st["emit"](engobj)
        if st["dma"]:
            insts = out if isinstance(out, (list, tuple)) else [out]
            assert len(insts) == st["dma"], f"{st['tag']}: dma count mismatch"
            for ins in insts:
                ins.then_inc(sems[("dmap",) + st["dma_sem"]], 16)
        else:
            assert out is not None, f"step {st['tag']} returned None"
            out.then_inc(sems[("comp", eng_name)], 1)


def blk_lhsT(w):
    O, Din = w.shape
    r = w.reshape(O // 128, 128, Din // 128, 128)
    return np.ascontiguousarray(r.transpose(0, 3, 2, 1))


def blk_ff2(w):
    r = w.reshape(16, 128, 64, 128)
    return np.ascontiguousarray(r.transpose(0, 3, 2, 1))


def blk_x(xb):
    r = xb.reshape(4, 512, 16, 128)
    return np.ascontiguousarray(r.transpose(2, 0, 3, 1))


def unblk_out(o):
    return np.ascontiguousarray(o.transpose(1, 3, 0, 2).reshape(S, D))


def pack_vec(v, n):
    return np.ascontiguousarray(v.reshape(n, 128).T)


def build_program():
    nc = bass.Bass("TRN2", target_bir_lowering=False)
    phases = DBG["phases"]

    xT = nc.dram_tensor("xT", [NC, NTC, 128, 512], F32, kind="ExternalInput")
    cattn = nc.dram_tensor("cattn", [18, 128, NC, 128], F32, kind="ExternalInput")
    cproj = nc.dram_tensor("cproj", [NC, 128, NC, 128], F32, kind="ExternalInput")
    wq_i = nc.dram_tensor("wq", [NC, 128, NC, 128], F32, kind="ExternalInput")
    wk_i = nc.dram_tensor("wk", [NC, 128, NC, 128], F32, kind="ExternalInput")
    wv_i = nc.dram_tensor("wv", [NC, 128, NC, 128], F32, kind="ExternalInput")
    wo_i = nc.dram_tensor("wo", [NC, 128, NC, 128], F32, kind="ExternalInput")
    ff1_i = nc.dram_tensor("ff1", [FFC, 128, NC, 128], F32, kind="ExternalInput")
    ff2_i = nc.dram_tensor("ff2", [NC, 128, FFC, 128], F32, kind="ExternalInput")
    wdn_i = nc.dram_tensor("wdn", [128, NC, RLOW], F32, kind="ExternalInput")
    wup_i = nc.dram_tensor("wup", [RLOW, NC, 128], F32, kind="ExternalInput")
    wrt_i = nc.dram_tensor("wrt", [128, NC, 3], F32, kind="ExternalInput")
    vecs = nc.dram_tensor("vecs", [128, 114], F32, kind="ExternalInput")
    outT = nc.dram_tensor("outT", [NC, NB, 128, 512], F32, kind="ExternalOutput")

    def dram(n, s, dt):
        kind = "ExternalOutput" if n in DBG["outs"] else "Internal"
        return nc.dram_tensor(n, s, dt, kind=kind)

    xn_s = dram("xn_s", [NC, NB, 128, 512], BF16)
    A_s = dram("A_s", [NC, NB, 128, 512], BF16)
    k2_s = dram("k2_s", [H, NB, 128, 512], BF16)
    q2_s = dram("q2_s", [H, NB, 128, 512], BF16)
    v2_s = dram("v2_s", [H, NC, 128, 128], BF16)
    ctx2_s = dram("ctx2_s", [H, NC, 128, 128], BF16)
    wo_img = dram("wo_img", [NC, 128, NC, 128], BF16)
    ff1_img = dram("ff1_img", [FFC, 128, NC, 128], BF16)
    ff2_img = dram("ff2_img", [NC, 128, FFC, 128], BF16)
    mrow_s = dram("mrow_s", [4, 2048], F32)
    t_s = dram("t_s", [RLOW, 2048], BF16)
    wupb_s = dram("wupb_s", [RLOW, NC, 128], BF16)

    iden16_h = nc.inline_tensor(np.eye(128, dtype=ml_dtypes.bfloat16), "iden16c")
    tri_np = np.where(np.arange(128)[None, :] <= np.arange(128)[:, None],
                      np.float32(0.0), np.float32(NEG))
    tri_h = nc.inline_tensor(tri_np, "tri_c")
    ones_h = nc.inline_tensor(np.ones((128, 4), np.float32), "ones_c")
    onesr_h = nc.inline_tensor(np.ones((1, 128), np.float32), "onesr_c")
    ones16_h = nc.inline_tensor(np.ones((128, 4), ml_dtypes.bfloat16), "o16_c")
    onesr16_h = nc.inline_tensor(np.ones((1, 128), ml_dtypes.bfloat16), "or16_c")
    onesT16_h = nc.inline_tensor(np.ones((128, 128), ml_dtypes.bfloat16), "oT16_c")
    onesT32_h = nc.inline_tensor(np.ones((128, 128), np.float32), "oT32_c")
    _cn = np.zeros((128, 4), np.float32)
    _cn[:, 0] = EPS
    _cn[:, 1] = SCALE
    _cn[:, 2] = 1.0 / 2048.0
    scalars_h = nc.inline_tensor(_cn, "scalars_c")

    sbA = nc.alloc_sbuf_tensor("sbA", [128, 53200], F32)
    sb16 = sbA.bitcast(BF16)

    def b16(off, cols, rows=128, p0=0):
        return sb16[p0:p0 + rows, off // 2: off // 2 + cols]

    def f32v(off, cols, rows=128, p0=0):
        return sbA[p0:p0 + rows, off // 4: off // 4 + cols]

    I = OFF_I
    iden16 = b16(I, 128)
    tri32 = f32v(I + 256, 128)
    ones16_col = b16(I + 768, 1)
    ones16_col4 = b16(I + 768, 4)
    onesr = f32v(I + 800, 128, rows=1)
    onesr16 = b16(I + 1312, 128, rows=1)
    vec_sb = f32v(I + 1568, 114)
    wrt_sb = f32v(I + 2024, 48)
    wdn_sb = b16(I + 2216, 512)
    small = f32v(I + 3240, 16)
    onesT16 = b16(I + 3328, 128)
    onesT32 = f32v(I + 3584, 128)
    junkb = b16(I + 4096, 128)
    sqb = f32v(I + 4352, 18)
    pmax_mq = f32v(I + 4480, 4)
    pmax_in = f32v(I + 4496, 4)

    def onesr16_at(p0):
        return onesT16[p0:p0 + 1, :]

    def onesr_at(p0):
        return onesT32[p0:p0 + 1, :]
    V_CATTN, V_CPROJ, V_SB, V_L1S, V_L1B, V_L2S, V_L2B = 0, 18, 34, 50, 66, 82, 98

    pscore = nc.alloc_psum_tensor("pscore", [128, 2048], F32)
    pacc = nc.alloc_psum_tensor("pacc", [128, 512], F32)
    putil = nc.alloc_psum_tensor("putil", [128, 512], F32)
    ptr = nc.alloc_psum_tensor("ptr", [128, 1024], BF16)
    pacc2 = nc.alloc_psum_tensor("pacc2", [128, 512], F32)

    def psv(i):
        return pscore[:, 512 * i:512 * (i + 1)], f"PS{i}"

    epsc = small[:, 4:5]
    epsr = small[0:1, 4:5]
    sclc = small[:, 5:6]
    inv_n = small[:, 6:7]
    inv_nr = small[0:1, 6:7]

    P = Prog()
    MM = dict(skip_group_check=True)

    def _ld_consts(sy):
        return [sy.dma_start(iden16, iden16_h[:, :]),
                sy.dma_start(tri32, tri_h[:, :]),
                sy.dma_start(ones16_col4, ones16_h[:, :]),
                sy.dma_start(onesr, onesr_h[:, :]),
                sy.dma_start(onesr16, onesr16_h[:, :]),
                sy.dma_start(onesT16, onesT16_h[:, :]),
                sy.dma_start(onesT32, onesT32_h[:, :]),
                sy.dma_start(vec_sb, vecs[:, :]),
                sy.dma_start(wrt_sb, wrt_i.rearrange("p c r -> p (c r)")),
                sy.dma_start(small[:, 4:8], scalars_h[:, :])]
    P.step("sp", _ld_consts, dma=10, writes=["consts", "epsc"], tag="consts")
    P.step("pool", lambda gp: gp.dma_start(
        wdn_sb, wdn_i.rearrange("p c r -> p (c r)")),
        dma=1, writes=["wdn"], tag="wdn")
    P.step("pool", lambda gp: gp.dma_start(wupb_s[:, :, :], wup_i[:, :, :]),
           dma=1, writes=["wup_s"], tag="wupcast")
    P.step("dve", (lambda ve: ve.tensor_scalar(
        sqb[:, 0:16], vec_sb[:, 0:16], sclc, None, ALU.mult)),
        reads=["consts", "epsc"], writes=["sqb"], tag="sqb")

    x16a = [b16(OFF_B, 2048), b16(OFF_B + 4096, 2048)]
    xsq16 = b16(OFF_B + 8192, 2048)
    xrow = b16(OFF_B + 8192, 2048, rows=1)
    xf = [f32v(OFF_B + 16384, 512), f32v(OFF_B + 18432, 512)]
    xnf = [f32v(OFF_B + 20480, 512), f32v(OFF_B + 22528, 512)]
    tmid = b16(OFF_B + 24576, 2048, rows=RLOW)
    mu_bc = f32v(OFF_G, 512)
    rs_bc = f32v(OFF_G + 2048, 512)
    r_mu = b16(OFF_H, 2048, rows=1)
    r_var = b16(OFF_H + 4096, 2048, rows=1)
    r_rstd = b16(OFF_H + 8192, 2048, rows=1)
    r_rstd1 = b16(OFF_H + 12288, 2048, rows=1)
    r_mu2 = f32v(OFF_H, 512, rows=1)
    r_rs2 = f32v(OFF_H + 2048, 512, rows=1)

    def xn_slot(dc):
        return b16(OFF_A + 4096 * dc, 2048), f"A{dc}"

    def q_slot(oc):
        return b16(OFF_B + 4096 * oc, 2048), f"B{oc}"

    AKEYS = [f"A{d}:{t}" for d in range(NC) for t in range(NTC)]

    if phases >= 1:
        for dc in range(NC):
            sl = dc % 2
            P.step("pool", (lambda gp, dc=dc, sl=sl: gp.dma_start(
                x16a[sl].rearrange("p (t f) -> p t f", t=4),
                xT.rearrange("c t p f -> c p t f")[dc])),
                dma=1, writes=[f"x16:{sl}"], tag=f"p1a_ld{dc}")
            P.step("act", (lambda sc, sl=sl: sc.activation(
                xsq16, x16a[sl], AF.Square)),
                reads=[f"x16:{sl}"], writes=["xsq"], tag=f"p1a_sq{dc}")

            def mmx(te, dc=dc, sl=sl):
                last = None
                for tc in range(NTC):
                    last = te.matmul(pscore[0:1, 512 * tc:512 * tc + 512],
                                     ones16_col, x16a[sl][:, 512 * tc:512 * tc + 512],
                                     start=(dc == 0), stop=(dc == NC - 1), **MM)
                return last
            P.step("pe", mmx, reads=[f"x16:{sl}", "consts"],
                   writes=(["PS0", "PS1", "PS2", "PS3"]
                           if dc in (0, NC - 1) else []), tag=f"p1a_mmx{dc}")

            def mmx2(te, dc=dc):
                dsts = [putil[0:1, :], pacc[0:1, :], pacc2[0:1, :],
                        ptr.bitcast(F32)[0:1, 0:512]]
                last = None
                for tc in range(NTC):
                    last = te.matmul(dsts[tc], ones16_col,
                                     xsq16[:, 512 * tc:512 * tc + 512],
                                     start=(dc == 0), stop=(dc == NC - 1), **MM)
                return last
            P.step("pe", mmx2, reads=["xsq"],
                   writes=(["PUTIL", "PACC", "PACC2", "PTR"]
                           if dc in (0, NC - 1) else []), tag=f"p1a_mmq{dc}")

        def _stats(ve):
            for tc in range(NTC):
                ve.tensor_scalar(r_mu[:, 512 * tc:512 * tc + 512],
                                 pscore[0:1, 512 * tc:512 * tc + 512],
                                 inv_nr, None, ALU.mult)
            srcs = [putil, pacc, pacc2, ptr.bitcast(F32)]
            for tc in range(NTC):
                ve.tensor_scalar(r_var[:, 512 * tc:512 * tc + 512],
                                 srcs[tc][0:1, 0:512], inv_nr, None, ALU.mult)
            ve.tensor_tensor(xrow, r_mu, r_mu, ALU.mult)
            ve.tensor_copy(junkb, iden16)
            ve.tensor_copy(junkb, iden16)
            return ve.tensor_tensor(r_var, r_var, xrow, ALU.subtract)
        P.step("dve", _stats,
               reads=["PS0", "PS1", "PS2", "PS3", "PUTIL", "PACC", "PACC2",
                      "PTR", "epsc"],
               writes=["r_mu", "r_var", "xsq"], tag="p1_stats")

        def _rstd(sc):
            sc.activation(xrow, r_var, AF.Ln, bias=epsr)
            sc.copy(junkb, iden16)
            sc.copy(junkb, iden16)
            return sc.activation(r_rstd, xrow, AF.Exp, scale=-0.5)
        P.step("act", _rstd, reads=["r_var", "epsc"], writes=["r_rstd", "xsq"],
               tag="p1_rstd")

        def _v1(ve):
            ve.tensor_tensor(xrow, r_rstd, r_rstd, ALU.mult)
            ve.tensor_copy(junkb, iden16)
            ve.tensor_copy(junkb, iden16)
            return ve.tensor_tensor(r_rstd1, r_var, xrow, ALU.mult)
        P.step("dve", _v1, reads=["r_rstd", "r_var"],
               writes=["r_rstd1", "xsq"], tag="p1_v1")

        def _rstd1(sc):
            sc.activation(xrow, r_rstd1, AF.Ln, bias=epsr)
            sc.copy(junkb, iden16)
            sc.copy(junkb, iden16)
            return sc.activation(r_rstd1, xrow, AF.Exp, scale=-0.5)
        P.step("act", _rstd1, reads=["r_rstd1", "epsc"],
               writes=["r_rstd1", "xsq"], tag="p1_rstd1")

        for tc in range(NTC):
            def _bc(te, tc=tc):
                te.matmul(putil[:, :], onesr16_at(0),
                          r_mu[:, 512 * tc:512 * tc + 512], start=True,
                          stop=True, **MM)
                return te.matmul(pacc[:, :], onesr16_at(0),
                                 r_rstd[:, 512 * tc:512 * tc + 512],
                                 start=True, stop=True, **MM)
            P.step("pe", _bc, reads=["r_mu", "r_rstd", "consts"],
                   writes=["PUTIL", "PACC"], tag=f"p1b_bc{tc}")

            def _bce(ve):
                ve.tensor_copy(mu_bc, putil[:, :])
                return ve.tensor_copy(rs_bc, pacc[:, :])
            P.step("dve", _bce, reads=["PUTIL", "PACC"],
                   writes=["mu_bc", "rs_bc"], tag=f"p1b_bce{tc}")

            for dc in range(NC):
                sl = dc % 2
                P.step("sp", (lambda sy, dc=dc, tc=tc, sl=sl: sy.dma_start(
                    xf[sl], xT[dc, tc])),
                    dma=1, writes=[f"xf:{sl}"], tag=f"p1b_ld{dc}_{tc}")

                def _xn(ve, dc=dc, tc=tc, sl=sl):
                    ve.tensor_tensor(xnf[sl], xf[sl], mu_bc, ALU.subtract)
                    ve.tensor_tensor(xnf[sl], xnf[sl], rs_bc, ALU.mult)
                    xn, _ = xn_slot(dc)
                    return ve.tensor_copy(xn[:, 512 * tc:512 * tc + 512],
                                          xnf[sl])
                P.step("dve", _xn, reads=[f"xf:{sl}", "mu_bc", "rs_bc"],
                       writes=[f"xnf:{sl}", f"A{dc}:{tc}"],
                       tag=f"p1b_xn{dc}_{tc}")

                def _rt(te, dc=dc, tc=tc, sl=sl):
                    return te.matmul(pscore[0:3, 512 * tc:512 * tc + 512],
                                     wrt_sb[:, 3 * dc:3 * dc + 3], xnf[sl],
                                     start=(dc == 0), stop=(dc == NC - 1),
                                     **MM)
                P.step("pe", _rt, reads=[f"xnf:{sl}", "consts"],
                       writes=([f"PS{tc}"] if dc in (0, NC - 1) else []),
                       tag=f"p1b_rt{dc}_{tc}")

        for dc in range(NC):
            xn, _ = xn_slot(dc)
            P.step("pool", (lambda gp, dc=dc, xn=xn: [
                gp.dma_start(xn_s[dc, t], xn[:, 512 * t:512 * t + 512])
                for t in range(NB)]),
                dma=NB, reads=[f"A{dc}:{t}" for t in range(NTC)],
                writes=[f"xn_s:{dc}"], tag=f"p1_spxn{dc}")

        for tc in range(NTC):
            def _lr(te, tc=tc):
                last = None
                for dc in range(NC):
                    xn, _ = xn_slot(dc)
                    last = te.matmul(putil[0:RLOW, :],
                                     wdn_sb[:, RLOW * dc:RLOW * dc + RLOW],
                                     xn[:, 512 * tc:512 * tc + 512],
                                     start=(dc == 0), stop=(dc == NC - 1), **MM)
                return last
            P.step("pe", _lr,
                   reads=["wdn"] + [f"A{d}:{tc}" for d in range(NC)],
                   writes=["PUTIL"], tag=f"p1_lr{tc}")
            P.step("dve", (lambda ve, tc=tc: ve.tensor_copy(
                tmid[:, 512 * tc:512 * tc + 512], putil[0:RLOW, :])),
                reads=["PUTIL"], writes=[f"tmid:{tc}", "x16:0", "x16:1"],
                tag=f"p1_lre{tc}")
        P.step("pool", (lambda gp: gp.dma_start(t_s[:, :], tmid)),
               dma=1, reads=[f"tmid:{t}" for t in range(NTC)], writes=["t_s"],
               tag="p1_spt")

        lrow = f32v(OFF_B + 32768, 2048, rows=3)
        l0 = f32v(OFF_B + 40960, 2048, rows=1)
        l1 = f32v(OFF_B + 49152, 2048, rows=1)
        l2 = f32v(OFF_B + 57344, 2048, rows=1)
        m0 = f32v(OFF_B, 2048, rows=1)
        m1 = f32v(OFF_B + 8192, 2048, rows=1)
        m2 = f32v(OFF_B + 16384, 2048, rows=1)
        tt1 = f32v(OFF_B + 24576, 2048, rows=1)

        def _lev(ve):
            last = None
            for tc in range(NTC):
                last = ve.tensor_copy(lrow[:, 512 * tc:512 * tc + 512],
                                      pscore[0:3, 512 * tc:512 * tc + 512])
            return last
        P.step("dve", _lev, reads=["PS0", "PS1", "PS2", "PS3"],
               writes=["lrow"], tag="p1_lev")
        P.step("pool", (lambda gp: [gp.dma_start(l0, lrow[0:1, :]),
                                    gp.dma_start(l1, lrow[1:2, :]),
                                    gp.dma_start(l2, lrow[2:3, :])]),
               dma=3, reads=["lrow"], writes=["lr"], tag="p1_lmv")

        c_rows = [f32v(OFF_B + 8192 * i, 2048, rows=1, p0=64) for i in range(6)]

        def _mask(ve):
            ve.tensor_tensor(c_rows[0], l0, l1, ALU.is_ge)
            ve.tensor_tensor(c_rows[1], l0, l2, ALU.is_ge)
            ve.tensor_tensor(c_rows[2], l1, l0, ALU.is_gt)
            ve.tensor_tensor(c_rows[3], l1, l2, ALU.is_ge)
            ve.tensor_tensor(c_rows[4], l2, l0, ALU.is_gt)
            ve.tensor_tensor(c_rows[5], l2, l1, ALU.is_gt)
            ve.tensor_tensor(m0, c_rows[0], c_rows[1], ALU.mult)
            ve.tensor_tensor(m1, c_rows[2], c_rows[3], ALU.mult)
            return ve.tensor_tensor(m2, c_rows[4], c_rows[5], ALU.mult)
        P.step("dve", _mask, reads=["lr"],
               writes=["mrows", "x16:0", "x16:1", "xsq", "tmid:0", "tmid:1",
                       "tmid:2", "tmid:3"], tag="p1_mask")
        P.step("pool", (lambda gp: [gp.dma_start(mrow_s[0:1, :], m0),
                                    gp.dma_start(mrow_s[1:2, :], m1),
                                    gp.dma_start(mrow_s[2:3, :], m2)]),
               dma=3, reads=["mrows"], writes=["mrow_s"], tag="p1_msp")

    wband = [b16(OFF_D + 4096 * i, 2048) for i in range(3)]
    s32 = [f32v(OFF_E + 2048 * i, 512) for i in range(2)]
    vstage = b16(OFF_C + 5120, 2048)
    vtstage = b16(OFF_E + 8192, 512)
    k_sb = b16(OFF_F, 2048)
    v_tok = b16(OFF_F + 4096, 2048)

    def band_load(img, oc, cast, tag, reads=(), extra_writes=()):
        slot = oc % 3
        eng = "pool" if cast else "sp"
        P.step(eng, (lambda e, img=img, oc=oc, slot=slot: e.dma_start(
            wband[slot], img.rearrange("o p c f -> o p (c f)")[oc])),
            dma=1, reads=list(reads),
            writes=[f"wband:{slot}"] + list(extra_writes), tag=tag)
        return slot

    def proj_pair(tag, oc, slot, rhs_of, rhs_keys, regions, sink, pair):
        def _ch(te, pair=pair, slot=slot):
            last = None
            for dc in range(NC):
                rhs = rhs_of(dc)
                for i in range(2):
                    tc = 2 * pair + i
                    last = te.matmul(regions[i][0],
                                     wband[slot][:, 128 * dc:128 * dc + 128],
                                     rhs[:, 512 * tc:512 * tc + 512],
                                     start=(dc == 0), stop=(dc == NC - 1),
                                     **MM)
            return last
        P.step("pe", _ch, reads=[f"wband:{slot}"] + rhs_keys,
               writes=[regions[0][1], regions[1][1]],
               tag=f"{tag}_mm{oc}_{pair}")
        for i in range(2):
            sink(oc, 2 * pair + i, regions[i][0], regions[i][1])


    def proj_chain(tag, oc, slot, rhs_of, rhs_keys):
        def _chain(te, oc=oc, slot=slot):
            last = None
            for dc in range(NC):
                rhs = rhs_of(dc)
                for tc in range(NTC):
                    pv, _ = psv(tc)
                    last = te.matmul(pv, wband[slot][:, 128 * dc:128 * dc + 128],
                                     rhs[:, 512 * tc:512 * tc + 512],
                                     start=(dc == 0), stop=(dc == NC - 1), **MM)
            return last
        P.step("pe", _chain, reads=[f"wband:{slot}"] + rhs_keys,
               writes=["PS0", "PS1", "PS2", "PS3"], tag=tag)

    def v_transpose_spill(tc, dst_s, oc, src_key, tag, sink_sb=None):
        def _tr(te, tc=tc):
            last = None
            for i in range(4):
                last = te.transpose(ptr[:, 128 * i:128 * i + 128],
                                    vstage[:, 512 * tc + 128 * i:
                                           512 * tc + 128 * i + 128], iden16)
            return last
        P.step("pe", _tr, reads=[src_key, "consts"],
               writes=["PTRmq"], tag=f"{tag}tr")
        if sink_sb is not None:
            def _tre(ve, tc=tc):
                last = None
                for i in range(4):
                    t = 4 * tc + i
                    last = ve.tensor_copy(sink_sb[:, 128 * t:128 * t + 128],
                                          ptr[:, 128 * i:128 * i + 128])
                return last
            P.step("dve", _tre, reads=["PTRmq"],
                   writes=[f"vtok:{tc}"], tag=f"{tag}te")
        else:
            def _tre(ve):
                last = None
                for i in range(4):
                    last = ve.tensor_copy(vtstage[:, 128 * i:128 * i + 128],
                                          ptr[:, 128 * i:128 * i + 128])
                return last
            P.step("dve", _tre, reads=["PTRmq"],
                   writes=["vt:stg"], tag=f"{tag}te")
            P.step("pool", (lambda gp, tc=tc, oc=oc: [
                gp.dma_start(dst_s[oc, 4 * tc + i],
                             vtstage[:, 128 * i:128 * i + 128])
                for i in range(4)]),
                dma=4, reads=["vt:stg"], writes=[f"v2_s:{oc}"], tag=f"{tag}sp")

    if phases >= 2:
        def p2_sink(oc, tc, pv, pk):
            alias = []
            if oc < 16:
                dst, dkey = q_slot(oc)[0], f"B{oc}:{tc}"
                if oc <= 5:
                    alias = ["mrows"]
                elif oc <= 7:
                    alias = ["mrows"] + [f"tmid:{t}" for t in range(NTC)]
                elif oc <= 9:
                    alias = ["lrow"]
                else:
                    alias = ["lr"]
            elif oc == 16:
                dst, dkey = k_sb, f"k_sb:{tc}"
            else:
                dst, dkey = vstage, f"vstage:{tc}"

            def _ev(sc, oc=oc, tc=tc, pv=pv, dst=dst):
                if oc < 16:
                    return sc.activation(dst[:, 512 * tc:512 * tc + 512],
                                         pv, AF.Identity,
                                         bias=sqb[:, oc:oc + 1], scale=SCALE)
                return sc.activation(dst[:, 512 * tc:512 * tc + 512], pv,
                                     AF.Identity,
                                     bias=vec_sb[:, V_CATTN + oc:
                                                 V_CATTN + oc + 1])
            P.step("act", _ev, reads=[pk, "consts", "sqb"],
                   writes=[dkey] + alias, tag=f"p2_ev{oc}_{tc}")
            if oc == 17:
                v_transpose_spill(tc, None, None, dkey, f"p2_v{tc}",
                                  sink_sb=v_tok)

        P2REG = [[(pscore[:, 0:512], "PS0"), (pscore[:, 512:1024], "PS1")],
                 [(pscore[:, 1024:1536], "PS2"), (pscore[:, 1536:2048], "PS3")]]
        for oc in [16, 17] + list(range(16)):
            slot = band_load(cattn, oc, True, f"p2_w{oc}")
            for pair in range(2):
                proj_pair("p2", oc, slot, lambda dc: xn_slot(dc)[0], AKEYS,
                          P2REG[pair], p2_sink, pair)

    if DBG.get("spy_qkv") and phases >= 2:
        spy1 = nc.dram_tensor("spy1", [18, 128, 2048], BF16,
                              kind="ExternalOutput")
        for oc in range(18):
            if oc < 16:
                srcb, keys = q_slot(oc)[0], [f"B{oc}:{t}" for t in range(NTC)]
            elif oc == 16:
                srcb, keys = k_sb, [f"k_sb:{t}" for t in range(NTC)]
            else:
                srcb, keys = v_tok, [f"vtok:{t}" for t in range(NTC)]
            P.step("pool", (lambda gp, oc=oc, srcb=srcb: gp.dma_start(
                spy1[oc], srcb)), dma=1, reads=keys, tag=f"spy1_{oc}")

    if phases >= 5:
        for tc in range(NTC):
            P.step("pe", (lambda te, tc=tc: te.matmul(
                putil[:, :], onesr16_at(0), r_rstd1[:, 512 * tc:512 * tc + 512],
                start=True, stop=True, **MM)),
                reads=["r_rstd1", "consts"], writes=["PUTIL"],
                tag=f"p3a_bc{tc}")
            P.step("dve", (lambda ve: ve.tensor_copy(rs_bc, putil[:, :])),
                   reads=["PUTIL"], writes=["rs_bc"], tag=f"p3a_bce{tc}")
            for dc in range(NC):
                P.step("dve", (lambda ve, dc=dc, tc=tc: ve.tensor_tensor(
                    s32[0], xn_slot(dc)[0][:, 512 * tc:512 * tc + 512], rs_bc,
                    ALU.mult)),
                    reads=[f"A{dc}:{tc}", "rs_bc"], writes=["s32:0"],
                    tag=f"p3a_m{dc}_{tc}")
                P.step("act", (lambda sc, dc=dc, tc=tc: sc.activation(
                    xn_slot(dc)[0][:, 512 * tc:512 * tc + 512], s32[0],
                    AF.Identity, bias=vec_sb[:, V_L1B + dc:V_L1B + dc + 1],
                    scale=vec_sb[:, V_L1S + dc:V_L1S + dc + 1])),
                    reads=["s32:0", "consts"], writes=[f"A{dc}:{tc}"],
                    tag=f"p3a_a{dc}_{tc}")

    Pbuf_mq = b16(OFF_C, 2048)
    PT_mq = b16(OFF_C + 9216, 2048)
    Pbuf_in = b16(OFF_C + 13312, 2048)
    PT_in = b16(OFF_C + 17408, 2048)
    k2h = [b16(OFF_H, 2048), b16(OFF_H + 4096, 2048)]
    v2h = [b16(OFF_H + 8192, 2048), b16(OFF_H + 12288, 2048)]
    q2h = [b16(OFF_G, 2048), b16(OFF_E + 9216, 2048)]
    ctstage = [b16(OFF_E + 8192 + 256 * i, 128) for i in range(2)]
    ptr_in = pacc2.bitcast(BF16)
    ptr_f32 = ptr.bitcast(F32)

    BUF_MQ = dict(P=Pbuf_mq, Pk="Pb_mq", PT=PT_mq, PTk="PTmq",
                  pacc=pacc, pacck="PACC", paccg="PACCg",
                  ptr=ptr, ptrk="PTRmq", pm=pmax_mq, pmk="pm_mq",
                  negm=small[:, 0:1], ell=small[:, 2:3], rell=small[:, 3:4],
                  nk="negm_mq", ek="ell_mq", rk="rell_mq")
    BUF_IN = dict(P=Pbuf_in, Pk="Pb_in", PT=PT_in, PTk="PTin",
                  pacc=putil, pacck="PUTIL", paccg="PUTILg",
                  ptr=ptr_in, ptrk="PTRin", pm=pmax_in, pmk="pm_in",
                  negm=small[:, 8:9], ell=small[:, 10:11],
                  rell=small[:, 11:12],
                  nk="negm_in", ek="ell_in", rk="rell_in")

    def emit_attn_tile(h, j, q_ap, qcol, q_keys, k_ap, k_keys, v_ap, v_keys,
                       sink, tag, bufs, extra_first=(), bank_base=0):
        K = 128 * (j + 1)
        nkc = (K + 511) // 512
        boff = 512 * bank_base
        pskeys = [f"PS{bank_base + i}" for i in range(nkc)]
        Pb, Pk = bufs["P"], bufs["Pk"]
        PTb, PTk = bufs["PT"], bufs["PTk"]
        exp_alias = []
        pacc_b, pacck, paccg = bufs["pacc"], bufs["pacck"], bufs["paccg"]
        ptr_b, ptrk = bufs["ptr"], bufs["ptrk"]
        negm_c, ell_c, rell_c = bufs["negm"], bufs["ell"], bufs["rell"]
        nk, ek, rk = bufs["nk"], bufs["ek"], bufs["rk"]

        pm, pmk = bufs["pm"], bufs["pmk"]
        if nkc == 1:
            def _sc(te):
                return te.matmul(pscore[:, boff:boff + K],
                                 q_ap[:, qcol:qcol + 128],
                                 k_ap[:, 0:K], start=True, stop=True, **MM)
            P.step("pe", _sc, reads=q_keys + k_keys + ["consts"],
                   writes=pskeys + list(extra_first), tag=f"{tag}sc{h}_{j}")

            def _msk(ve):
                ve.tensor_tensor(pscore[:, boff + 128 * j:boff + 128 * j + 128],
                                 pscore[:, boff + 128 * j:boff + 128 * j + 128],
                                 tri32, ALU.add)
                return ve.reduce_max(negm_c, pscore[:, boff:boff + K], AX.X,
                                     negate=True)
            P.step("dve", _msk, reads=pskeys, writes=[pskeys[0], nk],
                   tag=f"{tag}mx{h}_{j}")
        else:
            for kc in range(nkc):
                n = min(512, K - 512 * kc)
                lastc = kc == nkc - 1

                def _sck(te, kc=kc, n=n):
                    return te.matmul(pscore[:, boff + 512 * kc:
                                            boff + 512 * kc + n],
                                     q_ap[:, qcol:qcol + 128],
                                     k_ap[:, 512 * kc:512 * kc + n],
                                     start=True, stop=True, **MM)
                P.step("pe", _sck, reads=q_keys + k_keys + ["consts"],
                       writes=[f"PS{bank_base + kc}"] +
                              (list(extra_first) if kc == 0 else []),
                       tag=f"{tag}sc{h}_{j}_{kc}")

                def _pmx(ve, kc=kc, n=n, lastc=lastc):
                    if lastc:
                        ve.tensor_tensor(
                            pscore[:, boff + 128 * j:boff + 128 * j + 128],
                            pscore[:, boff + 128 * j:boff + 128 * j + 128],
                            tri32, ALU.add)
                    return ve.reduce_max(pm[:, kc:kc + 1],
                                         pscore[:, boff + 512 * kc:
                                                boff + 512 * kc + n],
                                         AX.X)
                P.step("dve", _pmx, reads=[f"PS{bank_base + kc}"],
                       writes=[pmk] + ([f"PS{bank_base + kc}"] if lastc
                                       else []),
                       tag=f"{tag}pm{h}_{j}_{kc}")

            def _cmb(ve):
                ve.tensor_copy(junkb, iden16)
                return ve.reduce_max(negm_c, pm[:, 0:nkc], AX.X, negate=True)
            P.step("dve", _cmb, reads=[pmk], writes=[nk],
                   tag=f"{tag}cmb{h}_{j}")

        def _exp(sc):
            sc.memzero(ell_c)
            return sc.activation(Pb[:, 0:K], pscore[:, boff:boff + K], AF.Exp,
                                 bias=negm_c, accum_out=ell_c)
        P.step("act", _exp, reads=pskeys + [nk],
               writes=[Pk, ek] + exp_alias, tag=f"{tag}ex{h}_{j}")

        def _nrm(ve):
            ve.reciprocal(rell_c, ell_c)
            ve.tensor_copy(junkb, iden16)
            ve.tensor_copy(junkb, iden16)
            return ve.tensor_scalar(Pb[:, 0:K], Pb[:, 0:K], rell_c, None,
                                    ALU.mult)
        P.step("dve", _nrm, reads=[Pk, ek], writes=[Pk, rk],
               tag=f"{tag}nr{h}_{j}")

        def back():
            ngrp = (j + 8) // 8
            for g in range(ngrp):
                k0 = 8 * g
                n = min(8, j + 1 - k0)

                def _tr(te, k0=k0, n=n):
                    last = None
                    for i in range(n):
                        last = te.transpose(ptr_b[:, 128 * i:128 * i + 128],
                                            Pb[:, 128 * (k0 + i):
                                               128 * (k0 + i) + 128], iden16)
                    return last
                P.step("pe", _tr, reads=[Pk, "consts"], writes=[ptrk],
                       tag=f"{tag}tr{h}_{j}_{g}")

                def _tre(ve, k0=k0, n=n):
                    return ve.tensor_copy(PTb[:, 128 * k0:128 * (k0 + n)],
                                          ptr_b[:, 0:128 * n])
                P.step("dve", _tre, reads=[ptrk], writes=[PTk],
                       tag=f"{tag}te{h}_{j}_{g}")

                def _pv(te, k0=k0, n=n):
                    last = None
                    for i in range(n):
                        kt = k0 + i
                        last = te.matmul(pacc_b[:, 0:128],
                                         v_ap[:, 128 * kt:128 * kt + 128],
                                         PTb[:, 128 * kt:128 * kt + 128],
                                         start=(kt == 0), stop=(kt == j), **MM)
                    return last
                P.step("pe", _pv, reads=[PTk, paccg] + v_keys,
                       writes=([pacck] if (g == 0 or g == ngrp - 1) else []),
                       tag=f"{tag}pv{h}_{j}_{g}")
            sink(h, j)
        return back

    cast_pieces = []
    if phases >= 8:
        for nm, src_t, dst, rows_per in (("wo", wo_i, wo_img, 4),
                                         ("ff1", ff1_i, ff1_img, 4),
                                         ("ff2", ff2_i, ff2_img, 1)):
            n0 = src_t.shape[0]
            for pi in range(n0 // rows_per):
                def mkc(nm=nm, src_t=src_t, dst=dst, pi=pi, rows_per=rows_per):
                    def emitc():
                        P.step("pool", (lambda gp: gp.dma_start(
                            dst[pi * rows_per:(pi + 1) * rows_per],
                            src_t[pi * rows_per:(pi + 1) * rows_per])),
                            dma=1, writes=[f"{nm}img:{pi}"],
                            tag=f"cast_{nm}{pi}")
                    return emitc
                cast_pieces.append(mkc())

    def emit_casts(n):
        for _ in range(n):
            if cast_pieces:
                cast_pieces.pop(0)()

    bandctr = [100]

    def next_band(img, cast, tag, reads=(), extra_writes=()):
        slot = bandctr[0] % 3
        bandctr[0] += 1
        eng = "pool" if cast else "sp"
        P.step(eng, (lambda e, img=img, slot=slot: e.dma_start(
            wband[slot], img)),
            dma=1, reads=list(reads),
            writes=[f"wband:{slot}"] + list(extra_writes), tag=tag)
        return slot

    if phases >= 3:
        def mqa_sink(h, j):
            def _ev(sc, h=h, j=j):
                dst, _ = q_slot(h)
                return sc.activation(dst[:, 128 * j:128 * j + 128],
                                     pacc[:, 0:128], AF.Identity)
            P.step("act", _ev, reads=["PACC"],
                   writes=[f"ctx:{h}:{j}", "PACCg"], tag=f"mqev{h}_{j}")

        def emit_mq(h, j):
            q_ap, _ = q_slot(h)
            return emit_attn_tile(h, j, q_ap, 128 * j, [f"B{h}:{j // 4}"],
                                  k_sb, [f"k_sb:{i}" for i in range(NTC)],
                                  v_tok, [f"vtok:{i}" for i in range(NTC)],
                                  mqa_sink, "mq", BUF_MQ)

        def q2_sink(oc, tc, pv, pk):
            P.step("act", (lambda sc, tc=tc, pv=pv: sc.activation(
                vstage[:, 512 * tc:512 * tc + 512], pv, AF.Identity,
                scale=SCALE)),
                reads=[pk], writes=[f"vstage:{tc}"], tag=f"p3q2ev{oc}_{tc}")
            P.step("pool", (lambda gp, oc=oc, tc=tc: gp.dma_start(
                q2_s[oc, tc], vstage[:, 512 * tc:512 * tc + 512])),
                dma=1, reads=[f"vstage:{tc}"], writes=[f"q2_s:{oc}"],
                tag=f"p3q2sp{oc}_{tc}")

        def k2_sink(oc, tc, pv, pk):
            P.step("act", (lambda sc, tc=tc, pv=pv: sc.activation(
                vstage[:, 512 * tc:512 * tc + 512], pv, AF.Identity)),
                reads=[pk], writes=[f"vstage:{tc}"], tag=f"p3k2ev{oc}_{tc}")
            P.step("pool", (lambda gp, oc=oc, tc=tc: gp.dma_start(
                k2_s[oc, tc], vstage[:, 512 * tc:512 * tc + 512])),
                dma=1, reads=[f"vstage:{tc}"], writes=[f"k2_s:{oc}"],
                tag=f"p3k2sp{oc}_{tc}")

        def v2_sink(oc, tc, pv, pk):
            P.step("act", (lambda sc, tc=tc, pv=pv: sc.activation(
                vstage[:, 512 * tc:512 * tc + 512], pv, AF.Identity)),
                reads=[pk], writes=[f"vstage:{tc}"], tag=f"p3v2ev{oc}_{tc}")
            v_transpose_spill(tc, v2_s, oc, f"vstage:{tc}", f"p3v2t{oc}_{tc}")

        CTXKEYS = [f"ctx:{d}:{jj}" for d in range(NC) for jj in range(NC)]
        REG_B = [(putil[:, :], "PUTIL"), (pacc2[:, :], "PACC2")]
        REG_C = [(pacc[:, :], "PACC"), (ptr_f32[:, 0:512], "PTRmq")]

        def A_sink(oc, tc, pv, pk):
            P.step("act", (lambda sc, oc=oc, tc=tc, pv=pv: sc.activation(
                vstage[:, 512 * tc:512 * tc + 512], pv, AF.Identity,
                bias=vec_sb[:, V_CPROJ + oc:V_CPROJ + oc + 1])),
                reads=[pk, "consts"],
                writes=[f"vstage:{tc}"], tag=f"p2c_ev{oc}_{tc}")
            P.step("act", (lambda sc, oc=oc, tc=tc: sc.dma_start(
                A_s[oc, tc], vstage[:, 512 * tc:512 * tc + 512])),
                dma=1, reads=[f"vstage:{tc}"], writes=[f"A_s:{oc}:{tc}"],
                tag=f"p2c_sp{oc}_{tc}")

        def inner_sink(h, j):
            sl2 = (h * NC + j) % 2

            def _ev(ve, sl2=sl2):
                return ve.tensor_copy(ctstage[sl2], putil[:, 0:128])
            P.step("dve", _ev, reads=["PUTIL"],
                   writes=[f"ct:stg{sl2}", "PUTILg"] +
                          (["vt:stg"] if h == 0 and j <= 1 else []),
                   tag=f"p4ev{h}_{j}")
            P.step("act", (lambda sc, h=h, j=j, sl2=sl2: sc.dma_start(
                ctx2_s[h, j], ctstage[sl2])),
                dma=1, reads=[f"ct:stg{sl2}"], writes=[f"ctx2_s:{h}:{j}"],
                tag=f"p4sp{h}_{j}")

        def emit_in_loads(h):
            sl = h % 2
            P.step("sp", (lambda sy, h=h, sl=sl: sy.dma_start(
                k2h[sl].rearrange("p (b f) -> p b f", b=4),
                k2_s.rearrange("a b p f -> a p b f")[h])),
                dma=1, reads=[f"k2_s:{h}"],
                writes=[f"k2h:{sl}"] +
                       ((["r_mu"] if sl == 0 else ["r_var"]) if h < 2 else []),
                tag=f"p4_ldk{h}")
            P.step("sp", (lambda sy, h=h, sl=sl: sy.dma_start(
                v2h[sl].rearrange("p (b f) -> p b f", b=16),
                v2_s.rearrange("a b p f -> a p b f")[h])),
                dma=1, reads=[f"v2_s:{h}"],
                writes=[f"v2h:{sl}"] +
                       ((["r_rstd"] if sl == 0 else ["r_rstd1", "r_mu2",
                                                     "r_rs2"]) if h < 2
                        else []),
                tag=f"p4_ldv{h}")
            P.step("sp", (lambda sy, h=h, sl=sl: sy.dma_start(
                q2h[sl].rearrange("p (b f) -> p b f", b=4),
                q2_s.rearrange("a b p f -> a p b f")[h])),
                dma=1, reads=[f"q2_s:{h}"],
                writes=[f"q2h:{sl}"] +
                       ((["mu_bc", "rs_bc"] if sl == 0 else
                         ["xnt:0", "xnt:1", "At:0", "At:1"]) if h < 2
                        else []),
                tag=f"p4_ldq{h}")

        def emit_in(h, j):
            sl = h % 2
            return emit_attn_tile(h, j, q2h[sl], 128 * j, [f"q2h:{sl}"],
                                  k2h[sl], [f"k2h:{sl}"], v2h[sl],
                                  [f"v2h:{sl}"], inner_sink, "in", BUF_IN,
                                  bank_base=(2 if j <= 7 else 0))

        do_inner = phases >= 7
        do_p3b = phases >= 6
        do_p2c = phases >= 4
        FILL_AT = {2: 0, 5: 1, 7: 2, 9: 3, 11: 4, 13: 5}
        pend = {"mq": None, "in": None}

        def drain(key):
            if pend[key] is not None:
                pend[key]()
                pend[key] = None

        for h in range(H):
            units = []
            if do_p3b:
                for which, img, sink in (("q2", wq_i, q2_sink),
                                         ("k2", wk_i, k2_sink),
                                         ("v2", wv_i, v2_sink)):
                    def mk_unit(which=which, img=img, sink=sink, h=h):
                        state = {}

                        def u0():
                            state["slot"] = next_band(
                                img.rearrange("o p c f -> o p (c f)")[h],
                                True, f"p3{which}w{h}")
                            proj_pair(f"p3{which}", h, state["slot"],
                                      lambda dc: xn_slot(dc)[0], AKEYS,
                                      REG_B, sink, 0)

                        def u1():
                            proj_pair(f"p3{which}", h, state["slot"],
                                      lambda dc: xn_slot(dc)[0], AKEYS,
                                      REG_B, sink, 1)
                        return [u0, u1]
                    units.extend(mk_unit())
            for j in range(NC):
                drain("mq")
                b = emit_mq(h, j)
                pend["mq"] = b
                if do_inner and h >= 1:
                    drain("in")
                    b2 = emit_in(h - 1, j)
                    pend["in"] = b2
                ui = FILL_AT.get(j)
                if ui is not None and ui < len(units):
                    units[ui]()
                if j == 14 and do_inner and do_p3b:
                    emit_in_loads(h)
            for u in units[6:]:
                u()
            emit_casts(2)
        drain("mq")

        cstate = {}

        def c_u(oc, pair):
            if pair == 0:
                cstate[oc] = next_band(
                    cproj.rearrange("o p c f -> o p (c f)")[oc], True,
                    f"p2c_w{oc}")
            proj_pair("p2c", oc, cstate[oc], lambda dc: q_slot(dc)[0],
                      CTXKEYS, REG_C, A_sink, pair)

        cu_list = []
        if do_p2c:
            cu_list = [(oc, pr) for oc in range(NC) for pr in (0, 1)]
        ci = 0
        if do_inner:
            for j in range(NC):
                drain("in")
                b2 = emit_in(15, j)
                pend["in"] = b2
                for _ in range(2):
                    if ci < len(cu_list):
                        c_u(*cu_list[ci])
                        ci += 1
            drain("in")
        while ci < len(cu_list):
            c_u(*cu_list[ci])
            ci += 1

    if DBG.get("spy_ctx") and phases >= 3:
        spy2 = nc.dram_tensor("spy2", [16, 128, 2048], BF16,
                              kind="ExternalOutput")
        for oc in range(16):
            P.step("pool", (lambda gp, oc=oc: gp.dma_start(
                spy2[oc], q_slot(oc)[0])),
                dma=1, reads=[f"ctx:{oc}:{j}" for j in range(16)],
                tag=f"spy2_{oc}")

    if phases >= 8:
        emit_casts(len(cast_pieces))

    if phases >= 8:
        tT_sb = b16(OFF_C, 2048, rows=RLOW)
        wup_sb = b16(OFF_C + 4096, 2048, rows=RLOW)
        P.step("sp", (lambda sy: sy.dma_start(tT_sb, t_s[:, :])),
               dma=1, reads=["t_s"], writes=["tTsb", "Pb_mq"], tag="t_tld")
        P.step("sp", (lambda sy: sy.dma_start(
            wup_sb, wupb_s.rearrange("r c f -> r (c f)"))),
            dma=1, reads=["wup_s"],
            writes=["wupsb"] +
                   [f"vstage:{t}" for t in range(NTC)], tag="t_wupld")

        def x2_slot(dc):
            return b16(OFF_B + 1024 * dc, 512), f"x2:{dc}"

        def xn2p_slot(dc):
            return b16(OFF_B + 16384 + 1024 * dc, 512), f"xn2p:{dc}"

        def h_slot(fc):
            return b16(OFF_A + 1024 * fc, 512), f"h:{fc}"

        def c2b_slot(dc):
            return b16(OFF_B + 36864 + 1024 * dc, 512), f"c2b:{dc}"

        C2OVER = [f"ctx:{d}:{jj}" for d in (9, 10, 11, 12) for jj in range(NC)]
        sqstage = b16(OFF_F, 512)
        xnt = [b16(OFF_E + 9216 + 1024 * i, 512) for i in range(2)]
        At = [b16(OFF_E + 11264 + 1024 * i, 512) for i in range(2)]
        EOVER = ["ct:stg0", "ct:stg1", "vt:stg"]
        mrow = f32v(OFF_H, 512, rows=1, p0=0)
        m0b = b16(OFF_B + 32768, 512)
        m1b = b16(OFF_B + 33792, 512)
        m2b = b16(OFF_B + 34816, 512)
        MBS = [m0b, m1b, m2b]

        def tail_band(img, idx, tag, imgkey):
            slot = idx % 3
            P.step("sp", (lambda sy, img=img, idx=idx, slot=slot: sy.dma_start(
                wband[slot], img.rearrange("o p c f -> o p (c f)")[idx])),
                dma=1, reads=[imgkey], writes=[f"wband:{slot}"], tag=tag)
            return slot

        for bt in range(NB):
            for mi in range(3):
                P.step("sp", (lambda sy, mi=mi, bt=bt: sy.dma_start(
                    mrow, mrow_s[mi:mi + 1, 512 * bt:512 * bt + 512])),
                    dma=1, reads=["mrow_s"], writes=["mrowt"],
                    tag=f"t{bt}_mld{mi}")
                P.step("pe", (lambda te: te.matmul(
                    putil[:, :], onesr_at(0), mrow, start=True, stop=True, **MM)),
                    reads=["mrowt", "consts"], writes=["PUTIL"],
                    tag=f"t{bt}_mbc{mi}")
                P.step("dve", (lambda ve, mi=mi: ve.tensor_copy(
                    MBS[mi], putil[:, :])),
                    reads=["PUTIL"],
                    writes=[f"mb:{mi}"], tag=f"t{bt}_mbe{mi}")

            for dc in range(NC):
                c2, c2k = c2b_slot(dc)
                P.step("sp", (lambda sy, dc=dc, bt=bt, c2=c2: [
                    sy.dma_start(c2[:, 128 * i:128 * i + 128],
                                 ctx2_s[dc, 4 * bt + i]) for i in range(4)]),
                    dma=4,
                    reads=[f"ctx2_s:{dc}:{4 * bt + i}" for i in range(4)],
                    writes=[c2k] + (C2OVER if bt == 0 else []),
                    tag=f"t{bt}_c2ld{dc}")

            for oc in range(NC):
                slot = tail_band(wo_img, oc, f"t{bt}_wow{oc}",
                                 f"woimg:{oc // 4}")

                def _chain(te, slot=slot):
                    last = None
                    for dc in range(NC):
                        last = te.matmul(pacc2[:, :],
                                         wband[slot][:, 128 * dc:128 * dc + 128],
                                         c2b_slot(dc)[0],
                                         start=(dc == 0), stop=(dc == NC - 1),
                                         **MM)
                    return last
                P.step("pe", _chain,
                       reads=[f"wband:{slot}"] +
                             [c2b_slot(d)[1] for d in range(NC)],
                       writes=["PACC2"], tag=f"t{bt}_womm{oc}")
                xsl = oc % 2
                P.step("sp", (lambda sy, oc=oc, bt=bt, xsl=xsl: sy.dma_start(
                    xnt[xsl], xn_s[oc, bt])),
                    dma=1, reads=[f"xn_s:{oc}"],
                    writes=[f"xnt:{xsl}"] + EOVER +
                           (["q2h:1"] if bt == 0 and oc < 2 else []),
                    tag=f"t{bt}_xnld{oc}")

                def _x2(ve, oc=oc, xsl=xsl):
                    return ve.tensor_tensor(x2_slot(oc)[0], pacc2[:, :],
                                            xnt[xsl], ALU.add)
                P.step("dve", _x2, reads=["PACC2", f"xnt:{xsl}"],
                       writes=[x2_slot(oc)[1]], tag=f"t{bt}_x2{oc}")
                P.step("act", (lambda sc, oc=oc: sc.activation(
                    sqstage, x2_slot(oc)[0], AF.Square)),
                    reads=[x2_slot(oc)[1]],
                    writes=["sqst"] + (["k_sb:0"] if bt == 0 and oc == 0
                                       else []),
                    tag=f"t{bt}_sq{oc}")

                def _st(te, oc=oc):
                    te.matmul(putil[0:1, :], ones16_col, x2_slot(oc)[0],
                              start=(oc == 0), stop=(oc == NC - 1), **MM)
                    return te.matmul(pscore[0:1, 0:512], ones16_col,
                                     sqstage,
                                     start=(oc == 0), stop=(oc == NC - 1), **MM)
                P.step("pe", _st,
                       reads=[x2_slot(oc)[1], "sqst", "consts"],
                       writes=(["PUTIL", "PS0"] if oc in (0, NC - 1) else []),
                       tag=f"t{bt}_st{oc}")

            def _l2r(ve):
                ve.tensor_scalar(r_mu2, putil[0:1, :], inv_nr, None,
                                 ALU.mult)
                ve.tensor_scalar(r_rs2, pscore[0:1, 0:512], inv_nr, None,
                                 ALU.mult)
                ve.tensor_tensor(s32[0][0:1, :], r_mu2, r_mu2, ALU.mult)
                ve.tensor_copy(junkb, iden16)
                ve.tensor_copy(junkb, iden16)
                return ve.tensor_tensor(r_rs2, r_rs2, s32[0][0:1, :],
                                        ALU.subtract)
            P.step("dve", _l2r, reads=["PUTIL", "PS0", "epsc"],
                   writes=["r_mu2", "r_rs2", "s32:0"] +
                          (["k2h:0"] if bt == 0 else []),
                   tag=f"t{bt}_l2r")

            def _l2s(sc):
                sc.activation(s32[0][0:1, :], r_rs2, AF.Ln, bias=epsr)
                sc.copy(junkb, iden16)
                sc.copy(junkb, iden16)
                return sc.activation(r_rs2, s32[0][0:1, :], AF.Exp, scale=-0.5)
            P.step("act", _l2s, reads=["r_rs2", "epsc"],
                   writes=["r_rs2", "s32:0"], tag=f"t{bt}_l2s")

            def _l2b(te):
                te.matmul(putil[:, :], onesr_at(0), r_mu2, start=True,
                          stop=True, **MM)
                return te.matmul(pacc2[:, :], onesr_at(0), r_rs2, start=True,
                                 stop=True, **MM)
            P.step("pe", _l2b, reads=["r_mu2", "r_rs2", "consts"],
                   writes=["PUTIL", "PACC2"], tag=f"t{bt}_l2b")

            def _l2be(ve):
                ve.tensor_copy(mu_bc, putil[:, :])
                return ve.tensor_copy(rs_bc, pacc2[:, :])
            P.step("dve", _l2be, reads=["PUTIL", "PACC2"],
                   writes=["mu_bc", "rs_bc"] +
                          (["q2h:0"] if bt == 0 else []), tag=f"t{bt}_l2be")

            for dc in range(NC):
                def _xa(ve, dc=dc):
                    ve.tensor_tensor(s32[0], x2_slot(dc)[0], mu_bc,
                                     ALU.subtract)
                    return ve.tensor_tensor(s32[0], s32[0], rs_bc, ALU.mult)
                P.step("dve", _xa, reads=[x2_slot(dc)[1], "mu_bc", "rs_bc"],
                       writes=["s32:0"], tag=f"t{bt}_xa{dc}")
                P.step("act", (lambda sc, dc=dc: sc.activation(
                    xn2p_slot(dc)[0], s32[0], AF.Identity,
                    bias=vec_sb[:, V_L2B + dc:V_L2B + dc + 1],
                    scale=vec_sb[:, V_L2S + dc:V_L2S + dc + 1])),
                    reads=["s32:0", "consts"], writes=[xn2p_slot(dc)[1]],
                    tag=f"t{bt}_xb{dc}")

            XN2PK = [xn2p_slot(d)[1] for d in range(NC)]
            for fc in range(FFC):
                slot = tail_band(ff1_img, fc, f"t{bt}_f1w{fc}",
                                 f"ff1img:{fc // 4}")
                pv, pvk = psv(fc % 3)

                def _chain(te, slot=slot, pv=pv):
                    last = None
                    for dc in range(NC):
                        last = te.matmul(pv,
                                         wband[slot][:, 128 * dc:128 * dc + 128],
                                         xn2p_slot(dc)[0],
                                         start=(dc == 0), stop=(dc == NC - 1),
                                         **MM)
                    return last
                P.step("pe", _chain, reads=[f"wband:{slot}"] + XN2PK,
                       writes=[pvk], tag=f"t{bt}_f1mm{fc}")
                P.step("act", (lambda sc, fc=fc, pv=pv: sc.activation(
                    h_slot(fc)[0], pv, AF.Gelu_apprx_tanh)),
                    reads=[pvk], writes=[h_slot(fc)[1]], tag=f"t{bt}_g{fc}")

            for dc in range(NC):
                pv, pvk = psv(dc % 3)
                for qb in range(4):
                    slotq = qb % 2
                    P.step("sp", (lambda sy, dc=dc, qb=qb, slotq=slotq:
                                  sy.dma_start(
                        wband[slotq],
                        ff2_img.rearrange("o p c f -> o p (c f)")[dc]
                        [:, 2048 * qb:2048 * qb + 2048])),
                        dma=1, reads=[f"ff2img:{dc}"],
                        writes=[f"wband:{slotq}"], tag=f"t{bt}_f2w{dc}_{qb}")

                    def _chq(te, qb=qb, slotq=slotq, pv=pv):
                        last = None
                        for fi in range(16):
                            fc = 16 * qb + fi
                            last = te.matmul(
                                pv, wband[slotq][:, 128 * fi:128 * fi + 128],
                                h_slot(fc)[0], start=(fc == 0),
                                stop=(fc == FFC - 1), **MM)
                        return last
                    P.step("pe", _chq,
                           reads=[f"wband:{slotq}"] +
                                 [h_slot(16 * qb + i)[1] for i in range(16)],
                           writes=([pvk] if qb in (0, 3) else []),
                           tag=f"t{bt}_f2mm{dc}_{qb}")

                P.step("pe", (lambda te, dc=dc, bt=bt: te.matmul(
                    pacc2[:, :], wup_sb[:, 128 * dc:128 * dc + 128],
                    tT_sb[:, 512 * bt:512 * bt + 512], start=True, stop=True,
                    **MM)),
                    reads=["wupsb", "tTsb"], writes=["PACC2"],
                    tag=f"t{bt}_low{dc}")

                xsl = dc % 2
                P.step("sp", (lambda sy, dc=dc, bt=bt, xsl=xsl: sy.dma_start(
                    xnt[xsl], xn_s[dc, bt])),
                    dma=1, reads=[f"xn_s:{dc}"], writes=[f"xnt:{xsl}"] + EOVER,
                    tag=f"t{bt}_axn{dc}")
                P.step("sp", (lambda sy, dc=dc, bt=bt, xsl=xsl: sy.dma_start(
                    At[xsl], A_s[dc, bt])),
                    dma=1, reads=[f"A_s:{dc}:{bt}"], writes=[f"At:{xsl}"],
                    tag=f"t{bt}_aA{dc}")
                xbslot = f32v(OFF_E + 4096 + 2048 * xsl, 512)
                P.step("sp", (lambda sy, dc=dc, bt=bt, xbslot=xbslot:
                              sy.dma_start(xbslot, xT[dc, bt])),
                    dma=1, writes=[f"xb:{xsl}"], tag=f"t{bt}_ax{dc}")

                def _asm(ve, dc=dc, xsl=xsl, pv=pv, xbslot=xbslot):
                    o = s32[1]
                    t = s32[0]
                    ve.tensor_tensor(t, pv, x2_slot(dc)[0], ALU.add)
                    ve.tensor_tensor(o, t, m2b, ALU.mult)
                    ve.tensor_tensor(t, pacc2[:, :], xnt[xsl], ALU.add)
                    ve.tensor_tensor(t, t, m1b, ALU.mult)
                    ve.tensor_tensor(o, o, t, ALU.add)
                    ve.tensor_scalar(t, xnt[xsl],
                                     vec_sb[:, V_SB + dc:V_SB + dc + 1], None,
                                     ALU.add)
                    ve.tensor_tensor(t, t, m0b, ALU.mult)
                    ve.tensor_tensor(o, o, t, ALU.add)
                    ve.tensor_tensor(t, xbslot, At[xsl], ALU.add)
                    return ve.tensor_tensor(o, t, o, ALU.add)
                P.step("dve", _asm,
                       reads=[pvk, x2_slot(dc)[1], "PACC2", f"xnt:{xsl}",
                              f"At:{xsl}", f"xb:{xsl}", "mb:0", "mb:1",
                              "mb:2", "consts"],
                       writes=["s32:1", "s32:0"], tag=f"t{bt}_asm{dc}")
                P.step("pool", (lambda gp, dc=dc, bt=bt: gp.dma_start(
                    outT[dc, bt], s32[1])),
                    dma=1, reads=["s32:1"], writes=["outw"],
                    tag=f"t{bt}_out{dc}")

    stack = contextlib.ExitStack()
    sems = {}
    for e in Prog.ENG:
        sems[("comp", e)] = stack.enter_context(nc.semaphore(f"c_{e}"))
        for i in range(Prog.POOLN[e]):
            if (e, i) in P.pool_uses or True:
                sems[("dmap", e, i)] = stack.enter_context(
                    nc.semaphore(f"d_{e}{i}"))
    waited = {e: {} for e in Prog.ENG}
    with stack:
        with nc.Block() as block:
            secmap = {"pe": block.tensor, "dve": block.vector,
                      "act": block.scalar, "pool": block.gpsimd,
                      "sp": block.sync}

            def mk(e):
                def _run(engobj, e=e):
                    _emit_engine(P, e, engobj, sems, waited)
                    for (ee, i), u in sorted(P.pool_uses.items()):
                        if ee == e:
                            engobj.wait_ge(sems[("dmap", ee, i)], 16 * u)
                return _run
            for e in ["sp", "pool", "pe", "dve", "act"]:
                secmap[e](mk(e))
    return nc


def make_inputs(L0_attn_hidden_states, L0_attn_c_attn_w, L0_attn_c_attn_b,
                L0_attn_c_proj_w, L0_attn_c_proj_b, L0_ffn_simple_bias,
                L0_ffn_w_down, L0_ffn_w_up, L0_ffn_w_router, L0_ffn_ln1_scale,
                L0_ffn_ln1_bias, L0_ffn_wq, L0_ffn_wk, L0_ffn_wv, L0_ffn_wo,
                L0_ffn_ln2_scale, L0_ffn_ln2_bias, L0_ffn_w_ff1, L0_ffn_w_ff2):
    f = lambda a: np.asarray(a, dtype=np.float32)
    x = f(L0_attn_hidden_states)
    vecs = np.concatenate([
        pack_vec(f(L0_attn_c_attn_b), 18), pack_vec(f(L0_attn_c_proj_b), 16),
        pack_vec(f(L0_ffn_simple_bias), 16), pack_vec(f(L0_ffn_ln1_scale), 16),
        pack_vec(f(L0_ffn_ln1_bias), 16), pack_vec(f(L0_ffn_ln2_scale), 16),
        pack_vec(f(L0_ffn_ln2_bias), 16)], axis=1)
    shared = {
        "cattn": blk_lhsT(f(L0_attn_c_attn_w)),
        "cproj": blk_lhsT(f(L0_attn_c_proj_w)),
        "wq": blk_lhsT(f(L0_ffn_wq)),
        "wk": blk_lhsT(f(L0_ffn_wk)),
        "wv": blk_lhsT(f(L0_ffn_wv)),
        "wo": blk_lhsT(f(L0_ffn_wo)),
        "ff1": blk_lhsT(f(L0_ffn_w_ff1)),
        "ff2": blk_ff2(f(L0_ffn_w_ff2)),
        "wdn": np.ascontiguousarray(
            f(L0_ffn_w_down).T.reshape(16, 128, RLOW).transpose(1, 0, 2)),
        "wup": np.ascontiguousarray(f(L0_ffn_w_up).T.reshape(RLOW, 16, 128)),
        "wrt": np.ascontiguousarray(
            f(L0_ffn_w_router).T.reshape(16, 128, 3).transpose(1, 0, 2)),
        "vecs": vecs,
    }
    return [dict(shared, xT=blk_x(x[b])) for b in range(B)]


def kernel(**inputs):
    in_maps = make_inputs(**inputs)
    nc = build_program()
    res = run_bass_kernel_spmd(nc, in_maps, core_ids=list(range(B)))
    DBG["res"] = res
    out = np.stack([unblk_out(r["outT"]) for r in res.results])
    return out.astype(np.float32)
